# version 47
# baseline (speedup 1.0000x reference)
import numpy as np
import ml_dtypes
from contextlib import ExitStack

import concourse.bass as bass
import concourse.mybir as mybir
from concourse.bass_utils import run_bass_kernel_spmd

F32 = mybir.dt.float32
BF16 = mybir.dt.bfloat16
I32 = mybir.dt.int32
AF = mybir.ActivationFunctionType
ALU = mybir.AluOpType
AX = mybir.AxisListType
BF = ml_dtypes.bfloat16

S = 2048
D = 1024
NT = 16
IH = 2816
NKC = 8
HEADS = 16
EPS = 1e-5

B_LO = 0x3F800000
B_HI = 0x40800000
B_MBOFF = 0x0B400000

GATE_CH = [(0, 512), (512, 512), (1024, 512), (1536, 512), (2048, 512), (2560, 256)]
REGW = [0]
for _c in GATE_CH:
    REGW.append(REGW[-1] + 8 * _c[1])
NOFC = IH // 128
NWT = IH // 128


class SemC:
    def __init__(self, handle, step=1):
        self.h = handle
        self.step = step
        self.n = 0


class Eng:
    def __init__(self, name):
        self.name = name
        self.ops = []

    def op(self, fn, waits=(), inc=None):
        tok = None
        if inc is not None:
            inc.n += inc.step
            tok = inc.n
        wl = [(s.h, v) for (s, v) in waits if v and v > 0]
        ih = inc.h if inc is not None else None
        istep = inc.step if inc is not None else 0

        def run(q):
            for h, v in wl:
                q.wait_ge(h, v)
            ins = fn(q)
            if ih is not None:
                ins.then_inc(ih, istep)
        self.ops.append(run)
        return tok


def build_program(debug_stop=None):
    nc = bass.Bass("TRN2", target_bir_lowering=False)

    x_in = nc.dram_tensor("x", [S, D], F32, kind="ExternalInput").ap()
    wg_in = nc.dram_tensor("wg", [IH, D], F32, kind="ExternalInput").ap()
    wu_in = nc.dram_tensor("wu", [IH, D], F32, kind="ExternalInput").ap()
    wd_in = nc.dram_tensor("wd", [D, IH], F32, kind="ExternalInput").ap()
    wqkvT_in = nc.dram_tensor("wqkvT", [D, 3 * D], F32, kind="ExternalInput").ap()
    woutT_in = nc.dram_tensor("woutT", [D, D], F32, kind="ExternalInput").ap()
    bout_in = nc.dram_tensor("bout", [1, D], F32, kind="ExternalInput").ap()
    gam_in = nc.dram_tensor("gam", [128, D], F32, kind="ExternalInput").ap()
    bet_in = nc.dram_tensor("bet", [128, D], F32, kind="ExternalInput").ap()
    out = nc.dram_tensor("out", [S, D], F32, kind="ExternalOutput").ap()

    xq_scr = nc.dram_tensor("xq_scr", [8 * 2048, 128], BF16, kind="Internal").ap()
    hq_scr = nc.dram_tensor("hq_scr", [4 * NOFC * 512, 128], BF16,
                            kind="Internal").ap()
    wgq_scr = nc.dram_tensor("wgq_scr", [8 * IH, 128], BF16, kind="Internal").ap()
    wuq_scr = nc.dram_tensor("wuq_scr", [8 * IH, 128], BF16, kind="Internal").ap()
    wdq_scr = nc.dram_tensor("wdq_scr", [2 * NOFC * 512, 128], BF16,
                            kind="Internal").ap()
    xn2_scr = nc.dram_tensor("xn2_scr", [NKC, S, 128], BF16, kind="Internal").ap()
    o_scr = nc.dram_tensor("o_scr", [NKC * S, 128], BF16, kind="Internal").ap()

    gstack = ExitStack()
    with gstack:
        def gsbt(name, shape, dtype):
            return gstack.enter_context(nc.sbuf_tensor(name, shape, dtype))

        X = gsbt("X", [128, NT * D], F32)
        IDN = gsbt("IDN", [128, 128], BF16)
        MASKB = gsbt("MASKB", [128, 128], BF16)
        ONES1 = gsbt("ONES1", [1, 128], BF16)
        EPS1 = gsbt("EPS1", [128, 1], F32)
        MV = gsbt("MV", [128, 2 * NT], F32)
        LNV = gsbt("LNV", [128, NT], F32)
        RSTD = gsbt("RSTD", [128, NT], F32)
        NEGMU = gsbt("NEGMU", [128, NT], F32)
        NMR = gsbt("NMR", [128, NT], F32)
        STT = gsbt("STT", [128, 12], F32)

        PSB = gstack.enter_context(nc.psum_tensor("PSB", [128, 2048], F32))
        PS4 = gstack.enter_context(nc.psum_tensor("PS4", [128, 512], F32))
        PS5 = gstack.enter_context(nc.psum_tensor("PS5", [128, 512], F32))
        PS6 = gstack.enter_context(nc.psum_tensor("PS6", [128, 512], F32))
        PS7 = gstack.enter_context(nc.psum_tensor("PS7", [128, 512], F32))
        PS4B = PS4[:].bitcast(BF16)
        PS7B = PS7[:].bitcast(BF16)

        P_t = SemC(gstack.enter_context(nc.semaphore("P_t")))
        P_v = SemC(gstack.enter_context(nc.semaphore("P_v")))
        P_s = SemC(gstack.enter_context(nc.semaphore("P_s")))
        P_g = SemC(gstack.enter_context(nc.semaphore("P_g")))
        D_s = SemC(gstack.enter_context(nc.semaphore("D_s")), step=16)
        D_a = SemC(gstack.enter_context(nc.semaphore("D_a")), step=16)
        D_g = SemC(gstack.enter_context(nc.semaphore("D_g")), step=16)

        sy, ve, sc, te, gp = Eng("sy"), Eng("ve"), Eng("sc"), Eng("te"), Eng("gp")
        ENGS = (sy, ve, sc, te, gp)

        st_ = {
            "qb_free_v": [0, 0, 0, 0],
            "qb_store_da": [0, 0, 0, 0],
            "qbp": 0,
            "xqf_free_t": 0,
            "wgf_free_t": [0, 0],
            "wdf_free_t": 0,
            "gps_free_s": [0, 0],
            "ups_free_v": [0, 0],
            "sgt_free_v": [0, 0],
            "ht_free_v": [0, 0],
            "down_free_v": 0,
            "hq_read_ds": 0,
            "u_tok": [0, 0, 0, 0],
            "x_tok_v": 0,
            "wq_tok_u": {},
            "wq_tok_d": {},
        }

        def all_counts():
            return [(P_t, P_t.n), (P_v, P_v.n), (P_s, P_s.n), (P_g, P_g.n),
                    (D_s, D_s.n), (D_a, D_a.n), (D_g, D_g.n)]

        def barrier():
            toks = all_counts()
            for e in ENGS:
                e.op(lambda q: q.nop(), waits=toks)

        sy.op(lambda q: q.dma_start(
            out=X[:].rearrange("p (t f) -> p t f", t=NT),
            in_=x_in.rearrange("(t p) f -> p t f", p=128)), inc=D_s)
        tok_xload = D_s.n

        ve.op(lambda q: q.memset(EPS1[:], EPS), inc=P_v)
        gp.op(lambda q: q.memset(IDN[:], 0.0), inc=P_g)
        gp.op(lambda q: q.affine_select(
            out=IDN[:], in_=IDN[:], compare_op=ALU.not_equal, fill=1.0,
            base=0, pattern=[[-1, 128]], channel_multiplier=1),
            waits=[(P_g, P_g.n)], inc=P_g)
        gp.op(lambda q: q.memset(MASKB[:], 0.0), waits=[(P_g, P_g.n)], inc=P_g)
        gp.op(lambda q: q.affine_select(
            out=MASKB[:], in_=MASKB[:], compare_op=ALU.is_ge, fill=-1e30,
            base=0, pattern=[[-1, 128]], channel_multiplier=1),
            waits=[(P_g, P_g.n)], inc=P_g)
        gp.op(lambda q: q.memset(ONES1[:], 1.0), waits=[(P_g, P_g.n)], inc=P_g)
        TOK_IDN = 2
        TOK_MASK = 4
        TOK_ONES = 5

        def emit_ln_prepass(want_nmr):
            xv = st_["x_tok_v"]
            for t in range(NT):
                c0 = t * D
                ve.op(lambda q, c0=c0: q.bn_stats(STT[:, 0:6], X[:, c0:c0 + 512]),
                      waits=[(D_s, tok_xload), (P_v, xv), (P_v, P_v.n)], inc=P_v)
                ve.op(lambda q, c0=c0: q.bn_stats(STT[:, 6:12], X[:, c0 + 512:c0 + 1024]),
                      waits=[(P_v, P_v.n)], inc=P_v)
                ve.op(lambda q, t=t: q.bn_aggr(MV[:, 2 * t:2 * t + 2], STT[:]),
                      waits=[(P_v, P_v.n)], inc=P_v)
            tok_mv = P_v.n
            mv3 = MV[:].rearrange("p (t two) -> p t two", two=2)
            varv = mv3[:, :, 1:2].squeeze(2)
            muv = mv3[:, :, 0:1].squeeze(2)
            sc.op(lambda q: q.activation(LNV[:], varv, AF.Ln, bias=EPS1[:]),
                  waits=[(P_v, tok_mv), (P_s, P_s.n)], inc=P_s)
            sc.op(lambda q: q.activation(RSTD[:], LNV[:], AF.Exp, scale=-0.5),
                  waits=[(P_s, P_s.n)], inc=P_s)
            tok_rstd = P_s.n
            ve.op(lambda q: q.tensor_scalar(NEGMU[:], muv, -1.0, None, ALU.mult),
                  waits=[(P_v, tok_mv), (P_v, P_v.n)], inc=P_v)
            if want_nmr:
                ve.op(lambda q: q.scalar_tensor_tensor(
                    NMR[:], muv, -1.0, RSTD[:], ALU.mult, ALU.mult),
                    waits=[(P_s, tok_rstd), (P_v, P_v.n)], inc=P_v)
            return tok_rstd, P_v.n

        def quant_steps(qb, src, W, src_waits, extra_ap=None, extra_const=None):
            nb = W // 16
            Qd = st_["Q"]
            QA, QB, QM, CAND = Qd["QA"], Qd["QB"], Qd["QM"], Qd["CAND"]
            Sq, Iq, Dq, Oq = (Qd["QS"][qb], Qd["QI"][qb], Qd["QD"][qb],
                              Qd["QO"][qb])
            off = (qb % 2) * 1024 if W <= 1024 else 0
            A = QA[:, off:off + W]
            Bq = QB[:, off:off + W]
            Mq = QM[:, off:off + W]
            src3 = src.rearrange("p (nb b) -> p nb b", b=16)
            steps = []

            def s_bm(pt):
                return ve.op(lambda q: q.tensor_reduce(
                    Sq[:, 0:nb], src3, AX.X, ALU.max, apply_absolute_value=True),
                    waits=list(src_waits) + [(P_v, st_["qb_free_v"][qb]),
                                             (P_v, pt)], inc=P_v)
            steps.append(s_bm)

            def s_scale(pt):
                return ve.op(lambda q: q.tensor_scalar(
                    Sq[:, 0:nb], Sq[:, 0:nb], 1.0 / 6.0, 1e-12, ALU.mult,
                    ALU.max), waits=[(P_v, pt)], inc=P_v)
            steps.append(s_scale)

            def s_recip(pt):
                return ve.op(lambda q: q.reciprocal(Iq[:, 0:nb], Sq[:, 0:nb]),
                             waits=[(P_v, pt)], inc=P_v)
            steps.append(s_recip)

            if extra_ap is not None or extra_const is not None:
                xop = extra_ap if extra_ap is not None else extra_const

                def s_dd(pt):
                    return ve.op(lambda q: q.tensor_scalar(
                        Dq[:, 0:nb], Sq[:, 0:nb], xop, None, ALU.mult),
                        waits=[(P_v, pt)], inc=P_v)
                steps.append(s_dd)
                dd = Dq
            else:
                dd = Sq
            inv3 = Iq[:, 0:nb].unsqueeze(2).broadcast_to([128, nb, 16])
            u3 = A.rearrange("p (nb b) -> p nb b", b=16)

            def s_u(pt):
                tok = ve.op(lambda q: q.tensor_tensor(u3, src3, inv3, ALU.mult),
                            waits=[(P_v, pt)], inc=P_v)
                st_["u_tok"][qb] = tok
                return tok
            steps.append(s_u)
            ui = A.bitcast(I32)
            mi = Mq.bitcast(I32)
            candv = CAND[:, 0:1].broadcast_to([128, W]).bitcast(I32)

            def s_and(pt):
                return ve.op(lambda q: q.tensor_tensor(
                    mi, ui, candv, ALU.bitwise_and), waits=[(P_v, pt)], inc=P_v)
            steps.append(s_and)

            def s_clamp(pt):
                return ve.op(lambda q: q.tensor_scalar(
                    Mq, Mq, 1.0, 4.0, ALU.max, ALU.min),
                    waits=[(P_v, pt)], inc=P_v)
            steps.append(s_clamp)

            def s_t1(pt):
                return ve.op(lambda q: q.scalar_tensor_tensor(
                    Bq, Mq, 6291456.0, A, ALU.mult, ALU.add),
                    waits=[(P_v, pt)], inc=P_v)
            steps.append(s_t1)

            def s_q(pt):
                return ve.op(lambda q: q.scalar_tensor_tensor(
                    A, Mq, -6291456.0, Bq, ALU.mult, ALU.add),
                    waits=[(P_v, pt)], inc=P_v)
            steps.append(s_q)
            dd3 = dd[:, 0:nb].unsqueeze(2).broadcast_to([128, nb, 16])
            o3 = Oq[:, 0:W].rearrange("p (nb b) -> p nb b", b=16)

            def s_xq(pt):
                return ve.op(lambda q: q.tensor_tensor(o3, u3, dd3, ALU.mult),
                             waits=[(P_v, pt),
                                    (D_a, st_["qb_store_da"][qb])], inc=P_v)
            steps.append(s_xq)
            return steps

        def run_chains(chains):
            toks = [0] * len(chains)
            idx = [0] * len(chains)
            done = 0
            while done < len(chains):
                done = 0
                for ci in range(len(chains)):
                    if idx[ci] < len(chains[ci]):
                        toks[ci] = chains[ci][idx[ci]](toks[ci])
                        idx[ci] += 1
                    else:
                        done += 1
            return toks

        def emit_ffn_sublayer(fs, first):
            XQF, HQF, WGF, WUF, WDF, WRAW, SGT = (
                fs["XQF"], fs["HQF"], fs["WGF"], fs["WUF"], fs["WDF"],
                fs["WRAW"], fs["SGT"])
            st_["Q"] = fs
            QB_, QO_ = fs["QB"], fs["QO"]
            HT = [QB_[:, 0:512], QB_[:, 1024:1536]]
            ve.op(lambda q: q.memset(fs["CAND"][:], float("inf")),
                  waits=[(P_v, P_v.n)], inc=P_v)
            tok_rstd, tok_negmu = emit_ln_prepass(want_nmr=False)

            wpairs = []
            for t in range(NWT):
                wpairs.append([("g", wg_in, wgq_scr, t, 0, D),
                               ("u", wu_in, wuq_scr, t, 0, D)])
            for t in range(D // 128):
                wpairs.append([("d", wd_in, wdq_scr, t, 0, 1024),
                               ("d", wd_in, wdq_scr, t, 1024, 1024)])
                wpairs.append([("d", wd_in, wdq_scr, t, 2048, 768)])
            wcur = [0]
            wraw_free = [0, 0, 0, 0]

            loadq = []

            def preload(idx):
                if idx >= len(wpairs):
                    return
                pair = wpairs[idx]
                toks = []
                for e, (kind, src_hbm, dst_scr, t, off, W) in enumerate(pair):
                    sc.op(lambda q, src_hbm=src_hbm, t=t, off=off, W=W,
                          e=e: q.dma_start(
                              out=WRAW[:, e * 1024:e * 1024 + W],
                              in_=src_hbm[t * 128:(t + 1) * 128,
                                          off:off + W]),
                          waits=[(P_v, wraw_free[e])], inc=D_a)
                    toks.append(D_a.n)
                toks = [toks[-1]] * len(toks)
                loadq.append(toks)

            def emit_wpairs(n):
                for _ in range(n):
                    if wcur[0] >= len(wpairs):
                        return
                    if not loadq:
                        preload(wcur[0])
                    pair = wpairs[wcur[0]]
                    base = (wcur[0] % 2) * 2
                    wcur[0] += 1
                    loads = loadq.pop(0)
                    chains = []
                    for e, (kind, src_hbm, dst_scr, t, off, W) in enumerate(pair):
                        qb = base + e
                        chains.append(quant_steps(
                            qb, WRAW[:, e * 1024:e * 1024 + W], W,
                            [(D_a, loads[e])],
                            extra_const=(0.5 if kind == "d" else None)))
                    toks = run_chains(chains)
                    for e in range(len(pair)):
                        wraw_free[e] = st_["u_tok"][base + e]
                    preload(wcur[0])
                    for e, (kind, src_hbm, dst_scr, t, off, W) in enumerate(pair):
                        qb = base + e
                        st_["qb_free_v"][qb] = toks[e]
                        nsl = W // 128
                        c0 = off // 128
                        if kind == "d":
                            dst = wdq_scr[(t // 4) * 11264:
                                          (t // 4 + 1) * 11264, :] \
                                .rearrange("(c p) f -> c p f", p=512) \
                                [c0:c0 + nsl,
                                 (t % 4) * 128:(t % 4) * 128 + 128, :]
                        else:
                            ci_w = min(t // 4, len(GATE_CH) - 1)
                            coff_w, cw_w = GATE_CH[ci_w]
                            dst = dst_scr[REGW[ci_w]:REGW[ci_w] + 8 * cw_w, :] \
                                .rearrange("(c p) f -> c p f", p=cw_w) \
                                [0:8, t * 128 - coff_w:t * 128 - coff_w + 128, :]
                        sc.op(lambda q, dst=dst, qb=qb, W=W: q.dma_start(
                                  out=dst.rearrange("c p f -> p c f"),
                                  in_=QO_[qb][:, 0:W]
                                  .rearrange("p (c f) -> p c f", f=128)),
                              waits=[(P_v, toks[e])], inc=D_a)
                        st_["qb_store_da"][qb] = D_a.n
                        if kind == "u":
                            st_["wq_tok_u"][t] = D_a.n
                        elif kind == "d":
                            st_["wq_tok_d"][t] = D_a.n

            if first:
                emit_wpairs(4)

            xqf_tok = [0] * NT
            for t0p in range(0, NT, 2):
                base = (t0p // 2 % 2) * 2
                chains = []
                srcs = []
                for e in range(2):
                    t = t0p + e
                    qb = base + e
                    off = e * 1024
                    src = QB_[:, off:off + D]
                    sc.op(lambda q, t=t, src=src: q.activation(
                        src, X[:, t * D:(t + 1) * D], AF.Identity,
                        bias=NEGMU[:, t:t + 1]),
                        waits=[(P_v, tok_negmu), (P_v, st_["qb_free_v"][qb]),
                               (P_s, P_s.n)], inc=P_s)
                    srcs.append(P_s.n)
                    chains.append(quant_steps(qb, src, D, [(P_s, srcs[e])],
                                              extra_ap=RSTD[:, t:t + 1]))
                toks = run_chains(chains)
                for e in range(2):
                    t = t0p + e
                    qb = base + e
                    st_["qb_free_v"][qb] = toks[e]
                    sc.op(lambda q, t=t, qb=qb: q.dma_start(
                        out=xq_scr[(t // 2) * 2048:(t // 2) * 2048 + 2048, :]
                        .rearrange("(c p) f -> c p f", p=256)
                        [0:8, (t % 2) * 128:(t % 2) * 128 + 128, :]
                        .rearrange("c p f -> p c f"),
                        in_=QO_[qb][:, 0:D].rearrange("p (c f) -> p c f", f=128)),
                        waits=[(P_v, toks[e])], inc=D_a)
                    st_["qb_store_da"][qb] = D_a.n
                tok_pair_store = D_a.n
                pr = t0p // 2
                sy.op(lambda q, pr=pr: q.dma_start(
                    out=XQF[:, pr * 2048:(pr + 1) * 2048],
                    in_=xq_scr[pr * 2048:(pr + 1) * 2048, :],
                    transpose=True),
                    waits=[(D_a, tok_pair_store), (P_t, st_["xqf_free_t"])],
                    inc=D_s)
                xqf_tok[t0p] = xqf_tok[t0p + 1] = D_s.n

            hq_done = [[0] * len(GATE_CH) for _ in range(2)]
            banksPS47 = [PS4[:, 0:512], PS5[:, 0:512], PS6[:, 0:512],
                         PS7[:, 0:512]]
            banksPSB = [PSB[:, 0:512], PSB[:, 512:1024], PSB[:, 1024:1536],
                        PSB[:, 1536:2048]]
            hqf_free = [0, 0]

            def emit_wdf(half, free_tok):
                wq_wait = st_["wq_tok_d"].get(D // 128 - 1, 0) if first else 0
                sy.op(lambda q, half=half: q.dma_start(
                    out=WDF[:, 0:NOFC * 512],
                    in_=wdq_scr[half * 11264:(half + 1) * 11264, :],
                    transpose=True),
                    waits=[(P_t, free_tok), (D_a, wq_wait)],
                    inc=D_s)
                return D_s.n

            def emit_down_group(sg, half, quad, wdf_tok, banks, bank_waits):
                tbase = sg * 8 + quad * 4
                tg = tbase // 4
                hq_tok = hq_done[sg][len(GATE_CH) - 1]
                pan_tok = [0] * 6
                def load_panel(p):
                    if p > 5:
                        return
                    w = (min(4 * p + 4, NOFC) - 4 * p) * 512
                    sy.op(lambda q, p=p, tg=tg, w=w: q.dma_start(
                        out=HQF[p % 2][:, 0:w],
                        in_=hq_scr[tg * 11264 + p * 2048:
                                   tg * 11264 + p * 2048 + w, :],
                        transpose=True),
                        waits=[(D_a, hq_tok), (P_t, hqf_free[p % 2])], inc=D_s)
                    pan_tok[p] = D_s.n
                load_panel(0)
                load_panel(1)
                for ofc in range(NOFC):
                    p = ofc // 4
                    col = (ofc % 4) * 512
                    for j in range(4):
                        w0 = [(D_s, pan_tok[p])] \
                            if (ofc % 4 == 0 and j == 0) else []
                        if ofc == 0 and j == 0:
                            w0 += [(D_s, wdf_tok)] + list(bank_waits)
                        te.op(lambda q, j=j, p=p, col=col, ofc=ofc, banks=banks:
                              q.matmul(
                                  banks[j],
                                  HQF[p % 2][:, col + j * 128:
                                              col + (j + 1) * 128],
                                  WDF[:, ofc * 512:(ofc + 1) * 512],
                                  start=(ofc == 0), stop=(ofc == NOFC - 1)),
                              waits=w0, inc=(P_t if (j == 3) else None))
                    if ofc % 4 == 3 or ofc == NOFC - 1:
                        hqf_free[p % 2] = P_t.n
                        load_panel(p + 2)
                mm_tok = P_t.n

                def evac():
                    for j in range(4):
                        t = tbase + j
                        xs = X[:, t * D + half * 512:t * D + half * 512 + 512]
                        ve.op(lambda q, xs=xs, banks=banks, j=j: q.tensor_tensor(
                            xs, banks[j], xs, ALU.add),
                            waits=[(P_t, mm_tok), (P_v, P_v.n)], inc=P_v)
                    return P_v.n
                return evac, mm_tok

            ds_ = {"ps47": 0, "wdf": 0}
            ev_g = [None] * 8
            mm_g = [0] * 8
            def emit_chunk_loads(sg, ci):
                coff, cw = GATE_CH[ci]
                wpar = ci % 2
                wq_wait = st_["wq_tok_u"].get((coff + cw) // 128 - 1, 0) \
                    if first else 0
                sy.op(lambda q, ci=ci, cw=cw, wpar=wpar: q.dma_start(
                    out=WGF[wpar][:, 0:8 * cw],
                    in_=wgq_scr[REGW[ci]:REGW[ci] + 8 * cw, :],
                    transpose=True),
                    waits=[(P_t, st_["wgf_free_t"][wpar]), (D_a, wq_wait)],
                    inc=D_s)
                sy.op(lambda q, ci=ci, cw=cw, wpar=wpar: q.dma_start(
                    out=WUF[wpar][:, 0:8 * cw],
                    in_=wuq_scr[REGW[ci]:REGW[ci] + 8 * cw, :],
                    transpose=True),
                    inc=D_s)
                return D_s.n

            tok_w_next = [0]
            for sg in range(2):
                for ci, (coff, cw) in enumerate(GATE_CH):
                    wpar = ci % 2
                    if sg == 0 and ci == 0:
                        tok_w = emit_chunk_loads(0, 0)
                    else:
                        tok_w = tok_w_next[0]
                    if ci < len(GATE_CH) - 1:
                        if first:
                            emit_wpairs(max(0, min(4 * ci + 8, NWT) - wcur[0]))
                        tok_w_next[0] = emit_chunk_loads(sg, ci + 1)
                    elif sg == 0:
                        tok_w_next[0] = emit_chunk_loads(1, 0)
                    tok_w_a = 0
                    for t0p in range(sg * 8, sg * 8 + 8, 2):
                        base = (t0p // 2 % 2) * 2
                        hsrc = []
                        for e in range(2):
                            t = t0p + e
                            ph = e
                            g_ps = PSB[:, ph * 512:ph * 512 + cw]
                            u_ps = PSB[:, 1024 + ph * 512:1024 + ph * 512 + cw]
                            for kc in range(NKC):
                                w0 = []
                                if kc == 0:
                                    w0 = [(D_s, tok_w), (D_a, tok_w_a),
                                          (D_s, xqf_tok[t]),
                                          (P_s, st_["gps_free_s"][ph]),
                                          (P_v, st_["ups_free_v"][ph])]
                                te.op(lambda q, kc=kc, t=t, cw=cw, g_ps=g_ps,
                                      wpar=wpar: q.matmul(
                                          g_ps,
                                          XQF[:, (t // 2) * 2048 + kc * 256 +
                                              (t % 2) * 128:
                                              (t // 2) * 2048 + kc * 256 +
                                              (t % 2) * 128 + 128],
                                          WGF[wpar][:, kc * cw:kc * cw + cw],
                                          start=(kc == 0), stop=(kc == NKC - 1)),
                                      waits=w0, inc=(P_t if kc == NKC - 1 else None))
                            for kc in range(NKC):
                                te.op(lambda q, kc=kc, t=t, cw=cw, u_ps=u_ps,
                                      wpar=wpar: q.matmul(
                                          u_ps,
                                          XQF[:, (t // 2) * 2048 + kc * 256 +
                                              (t % 2) * 128:
                                              (t // 2) * 2048 + kc * 256 +
                                              (t % 2) * 128 + 128],
                                          WUF[wpar][:, kc * cw:kc * cw + cw],
                                          start=(kc == 0), stop=(kc == NKC - 1)),
                                      inc=(P_t if kc == NKC - 1 else None))
                            tok_mm = P_t.n
                            if t == sg * 8 + 7:
                                st_["wgf_free_t"][wpar] = tok_mm
                            sc.op(lambda q, e=e, cw=cw, g_ps=g_ps: q.activation(
                                SGT[e][:, 0:cw], g_ps, AF.Silu),
                                waits=[(P_t, tok_mm), (P_v, st_["sgt_free_v"][e]),
                                       (P_s, P_s.n)], inc=P_s)
                            st_["gps_free_s"][ph] = P_s.n
                            tok_sg = P_s.n
                            ve.op(lambda q, e=e, cw=cw, u_ps=u_ps: q.tensor_tensor(
                                HT[e][:, 0:cw], SGT[e][:, 0:cw], u_ps, ALU.mult),
                                waits=[(P_s, tok_sg), (P_v, st_["ht_free_v"][e]),
                                       (P_v, P_v.n)], inc=P_v)
                            st_["ups_free_v"][ph] = P_v.n
                            st_["sgt_free_v"][e] = P_v.n
                            hsrc.append(P_v.n)
                        chains = [quant_steps(base + e, HT[e][:, 0:cw], cw,
                                              [(P_v, hsrc[e])]) for e in range(2)]
                        toks = run_chains(chains)
                        nsl = cw // 128
                        for e in range(2):
                            t = t0p + e
                            qb = base + e
                            st_["ht_free_v"][e] = toks[e]
                            st_["qb_free_v"][qb] = toks[e]
                            sc.op(lambda q, qb=qb, coff=coff, nsl=nsl, t=t, cw=cw:
                                  q.dma_start(
                                      out=hq_scr[(t // 4) * 11264:
                                                 (t // 4 + 1) * 11264, :]
                                      .rearrange("(c p) f -> c p f", p=512)
                                      [coff // 128:coff // 128 + nsl,
                                       (t % 4) * 128:(t % 4) * 128 + 128, :]
                                      .rearrange("c p f -> p c f"),
                                      in_=QO_[qb][:, 0:cw]
                                      .rearrange("p (c f) -> p c f", f=128)),
                                  waits=[(P_v, toks[e]), (D_s, st_["hq_read_ds"])],
                                  inc=D_a)
                            st_["qb_store_da"][qb] = D_a.n
                    if first:
                        emit_wpairs(4)
                    hq_done[sg][ci] = D_a.n
                    if sg == 1 and ci == 0:
                        ds_["wdf"] = emit_wdf(0, st_["wdf_free_t"])
                        ev_g[0], mm_g[0] = emit_down_group(
                            0, 0, 0, ds_["wdf"], banksPS47,
                            [(P_v, ds_["ps47"])])
                    elif sg == 1 and ci == 1:
                        ds_["ps47"] = ev_g[0]()
                        ev_g[1], mm_g[1] = emit_down_group(
                            0, 0, 1, ds_["wdf"], banksPS47,
                            [(P_v, ds_["ps47"])])
                    elif sg == 1 and ci == 2:
                        ds_["ps47"] = ev_g[1]()
                        ds_["wdf"] = emit_wdf(1, mm_g[1])
                        ev_g[2], mm_g[2] = emit_down_group(
                            0, 1, 0, ds_["wdf"], banksPS47,
                            [(P_v, ds_["ps47"])])
                    elif sg == 1 and ci == 3:
                        ds_["ps47"] = ev_g[2]()
                        ev_g[3], mm_g[3] = emit_down_group(
                            0, 1, 1, ds_["wdf"], banksPS47,
                            [(P_v, ds_["ps47"])])
                    elif sg == 1 and ci == 4:
                        ds_["ps47"] = ev_g[3]()
                        st_["x_half0_v"] = ds_["ps47"]
                if sg == 0 and first:
                    emit_wpairs(len(wpairs))
            st_["xqf_free_t"] = P_t.n

            psb_waits = [(P_s, st_["gps_free_s"][0]), (P_s, st_["gps_free_s"][1]),
                         (P_v, st_["ups_free_v"][0]), (P_v, st_["ups_free_v"][1])]
            tok_wdf = emit_wdf(0, mm_g[3])
            ev_g[4], mm_g[4] = emit_down_group(1, 0, 0, tok_wdf, banksPS47,
                                               [(P_v, ds_["ps47"])])
            ev_g[5], mm_g[5] = emit_down_group(1, 0, 1, tok_wdf, banksPSB,
                                               psb_waits)
            ps47f = ev_g[4]()
            tok_wdf = emit_wdf(1, mm_g[5])
            ev_g[6], mm_g[6] = emit_down_group(1, 1, 0, tok_wdf, banksPS47,
                                               [(P_v, ps47f)])
            psbf = ev_g[5]()
            ev_g[7], mm_g[7] = emit_down_group(1, 1, 1, tok_wdf, banksPSB,
                                               [(P_v, psbf)])
            ev_g[6]()
            ev_g[7]()
            st_["wdf_free_t"] = mm_g[7]
            st_["hq_read_ds"] = D_s.n
            st_["x_tok_v"] = P_v.n

        def emit_attention(ps):
            st_["x_half0_v"] = 0
            astack = ps

            def sbt(name, shape, dtype):
                return astack.enter_context(nc.sbuf_tensor(name, shape, dtype))
            WKB = sbt("WKB", [128, NKC * 1024], BF16)
            XN2F = sbt("XN2F", [128, NKC * 2048], BF16)
            KF = sbt("KF", [128, NKC * 2048], BF16)
            VT = sbt("VT", [128, NT * 1024], BF16)
            BOUTB = sbt("BOUTB", [1, D], BF16)
            R2 = sbt("R2", [128, 4], F32)
            RMX = sbt("RMX", [128, 16], F32)
            LP = sbt("LP", [128, 48], F32)
            BX = sbt("BX", [128, 3], F32)
            RL = sbt("RL", [128, 3], F32)
            NMX = sbt("NMX", [128, 1], F32)
            LL = sbt("LL", [128, 1], F32)
            inner = ExitStack()
            GAM = inner.enter_context(nc.sbuf_tensor("GAM", [128, D], F32))
            BET = inner.enter_context(nc.sbuf_tensor("BET", [128, D], F32))
            XN1 = [inner.enter_context(nc.sbuf_tensor(f"XN1{i}", [128, D], F32))
                   for i in range(2)]
            XN1B = [inner.enter_context(nc.sbuf_tensor(f"XN1B{i}", [128, D], F32))
                    for i in range(2)]
            XN2B = [inner.enter_context(nc.sbuf_tensor("XN2B0", [128, D], BF16))]
            XN2B.append(XN2B[0])

            tok_rstd1, tok_nmr1 = emit_ln_prepass(want_nmr=True)
            sc.op(lambda q: q.dma_start(out=GAM[:], in_=gam_in), inc=D_a)
            sc.op(lambda q: q.dma_start(out=BET[:], in_=bet_in), inc=D_a)
            tok_gb = D_a.n
            gp.op(lambda q: q.dma_start(out=BOUTB[:], in_=bout_in),
                  waits=[(P_g, P_g.n)], inc=D_g)
            tok_bout = D_g.n
            wqkv3 = wqkvT_in.rearrange("(c p) e -> p c e", p=128)
            gp.op(lambda q: q.dma_start(
                out=WKB[:].rearrange("p (c f) -> p c f", c=NKC),
                in_=wqkv3[:, :, D:2 * D]), inc=D_g)
            tok_wk = D_g.n

            xn1_free_s = [0, 0]
            xn2b_free_a = [0, 0]
            xn2_store_h = [0, 0]
            xn2_store_q = [0, 0, 0, 0]
            for t in range(NT):
                par = t % 2
                sc.op(lambda q, t=t, par=par: q.activation(
                    XN1[par][:], X[:, t * D:(t + 1) * D], AF.Identity,
                    scale=RSTD[:, t:t + 1], bias=NMR[:, t:t + 1]),
                    waits=[(P_s, tok_rstd1), (P_v, tok_nmr1),
                           (P_v, xn1_free_s[par]), (P_s, P_s.n)], inc=P_s)
                tok_x1 = P_s.n
                ve.op(lambda q, par=par: q.bn_stats(STT[:, 0:6], XN1[par][:, 0:512]),
                      waits=[(P_s, tok_x1), (P_v, P_v.n)], inc=P_v)
                ve.op(lambda q, par=par: q.bn_stats(
                    STT[:, 6:12], XN1[par][:, 512:1024]),
                    waits=[(P_v, P_v.n)], inc=P_v)
                ve.op(lambda q, par=par: q.bn_aggr(
                    R2[:, 2 * par:2 * par + 2], STT[:]),
                    waits=[(P_v, P_v.n)], inc=P_v)
                tok_st2 = P_v.n
                sc.op(lambda q, par=par: q.activation(
                    LNV[:, par:par + 1], R2[:, 2 * par + 1:2 * par + 2], AF.Ln,
                    bias=EPS1[:]),
                    waits=[(P_v, tok_st2), (P_s, P_s.n)], inc=P_s)
                sc.op(lambda q, par=par: q.activation(
                    BX[:, par:par + 1], LNV[:, par:par + 1], AF.Exp, scale=-0.5),
                    waits=[(P_s, P_s.n)], inc=P_s)
                tok_r2 = P_s.n
                ve.op(lambda q, par=par: q.scalar_tensor_tensor(
                    NMX[:], R2[:, 2 * par:2 * par + 1], -1.0,
                    BX[:, par:par + 1], ALU.mult, ALU.mult),
                    waits=[(P_s, tok_r2), (P_v, P_v.n)], inc=P_v)
                tok_nm2 = P_v.n
                sc.op(lambda q, par=par: q.activation(
                    XN1B[par][:], XN1[par][:], AF.Identity,
                    scale=BX[:, par:par + 1], bias=NMX[:]),
                    waits=[(P_v, tok_nm2), (P_s, P_s.n)], inc=P_s)
                tok_x2a = P_s.n
                xn1_free_s[par] = tok_x2a
                ve.op(lambda q, par=par: q.tensor_tensor(
                    XN1[par][:], XN1B[par][:], GAM[:], ALU.mult),
                    waits=[(P_s, tok_x2a), (D_a, tok_gb), (P_v, P_v.n)], inc=P_v)
                ve.op(lambda q, par=par: q.tensor_tensor(
                    XN2B[par][:], XN1[par][:], BET[:], ALU.add),
                    waits=[(P_v, P_v.n), (D_g, xn2b_free_a[par])], inc=P_v)
                tok_x2 = P_v.n
                gp.op(lambda q, t=t, par=par: q.dma_start(
                    out=xn2_scr[:, t * 128:(t + 1) * 128, :]
                    .rearrange("c p f -> p c f"),
                    in_=XN2B[par][:].rearrange("p (c f) -> p c f", f=128)),
                    waits=[(P_v, tok_x2)], inc=D_g)
                xn2b_free_a[par] = D_g.n
                if t % 4 == 3:
                    xn2_store_q[t // 4] = D_g.n
            xn2_store_h[1] = D_g.n
            xn2f_qtok = [0, 0, 0, 0]
            for tg4 in range(4):
                for kc in range(NKC):
                    sy.op(lambda q, kc=kc, tg4=tg4: q.dma_start(
                        out=XN2F[:, kc * 2048 + tg4 * 512:
                                 kc * 2048 + tg4 * 512 + 512],
                        in_=xn2_scr[kc, tg4 * 512:(tg4 + 1) * 512, :],
                        transpose=True),
                        waits=[(D_g, xn2_store_q[tg4])], inc=D_s)
                xn2f_qtok[tg4] = D_s.n
            inner.close()
            qstack = ExitStack()
            WTMP = qstack.enter_context(
                nc.sbuf_tensor("WTMP", [128, NKC * 1024], BF16))
            gp.op(lambda q: q.dma_start(
                out=WTMP[:].rearrange("p (c f) -> p c f", c=NKC),
                in_=wqkv3[:, :, 2 * D:3 * D]),
                waits=[(D_g, xn2_store_h[1])], inc=D_g)
            tok_wv = D_g.n

            QRING = [PSB[:, 0:512], PSB[:, 512:1024], PSB[:, 1024:1536],
                     PSB[:, 1536:2048], PS5[:, 0:512], PS6[:, 0:512]]
            ps_free = [0] * 6
            for tg in range(4):
                for kf in range(NKC):
                    pp = (tg * NKC + kf) % 6
                    ps = QRING[pp]
                    for kc in range(NKC):
                        w0 = []
                        if kc == 0:
                            w0 = [(D_g, tok_wk), (D_s, xn2f_qtok[tg]),
                                  (P_s, ps_free[pp])]
                        te.op(lambda q, kc=kc, kf=kf, tg=tg, ps=ps: q.matmul(
                            ps,
                            WKB[:, kc * 1024 + kf * 128:kc * 1024 + (kf + 1) * 128],
                            XN2F[:, kc * 2048 + tg * 512:kc * 2048 + (tg + 1) * 512],
                            start=(kc == 0), stop=(kc == NKC - 1)),
                            waits=w0, inc=(P_t if kc == NKC - 1 else None))
                    tok_mm = P_t.n
                    sc.op(lambda q, kf=kf, tg=tg, ps=ps: q.activation(
                        KF[:, kf * 2048 + tg * 512:kf * 2048 + (tg + 1) * 512],
                        ps, AF.Copy),
                        waits=[(P_t, tok_mm)], inc=P_s)
                    ps_free[pp] = P_s.n
            tok_k_done_t = P_t.n
            gp.op(lambda q: q.dma_start(
                out=WKB[:].rearrange("p (c f) -> p c f", c=NKC),
                in_=wqkv3[:, :, 0:D]),
                waits=[(P_t, tok_k_done_t)], inc=D_g)
            tok_wq = D_g.n
            for t in range(NT):
                for vc in range(2):
                    pp = (t * 2 + vc) % 6
                    ps = QRING[pp]
                    for kc in range(NKC):
                        w0 = []
                        if kc == 0:
                            w0 = [(D_g, tok_wv), (P_s, ps_free[pp])]
                        te.op(lambda q, kc=kc, t=t, vc=vc, ps=ps: q.matmul(
                            ps,
                            XN2F[:, kc * 2048 + t * 128:kc * 2048 + (t + 1) * 128],
                            WTMP[:, kc * 1024 + vc * 512:kc * 1024 + (vc + 1) * 512],
                            start=(kc == 0), stop=(kc == NKC - 1)),
                            waits=w0, inc=(P_t if kc == NKC - 1 else None))
                    tok_mm = P_t.n
                    sc.op(lambda q, t=t, vc=vc, ps=ps: q.activation(
                        VT[:, t * 1024 + vc * 512:t * 1024 + (vc + 1) * 512],
                        ps, AF.Copy),
                        waits=[(P_t, tok_mm)], inc=P_s)
                    ps_free[pp] = P_s.n
            tok_v_done_t = P_t.n
            x3q = XN2F[:].rearrange("p (c n) -> p c n", c=NKC)
            for t in range(NT):
                for qf in range(NKC):
                    pp = (t * 2 + qf // 4) % 6
                    ps = QRING[pp]
                    qq = qf % 4
                    for kc in range(NKC):
                        w0 = []
                        if kc == 0 and qq == 0:
                            w0 = [(D_g, tok_wq), (P_s, ps_free[pp])]
                        te.op(lambda q, kc=kc, t=t, qf=qf, ps=ps, qq=qq: q.matmul(
                            ps[:, qq * 128:(qq + 1) * 128],
                            WKB[:, kc * 1024 + qf * 128:kc * 1024 + (qf + 1) * 128],
                            XN2F[:, kc * 2048 + t * 128:kc * 2048 + (t + 1) * 128],
                            start=(kc == 0), stop=(kc == NKC - 1)),
                            waits=w0, inc=(P_t if kc == NKC - 1 else None))
                tok_mm = P_t.n
                for half in range(2):
                    pp = (t * 2 + half) % 6
                    ps = QRING[pp]
                    sc.op(lambda q, t=t, half=half, ps=ps: q.activation(
                        x3q[:, 4 * half:4 * half + 4, t * 128:(t + 1) * 128],
                        ps.rearrange("p (c n) -> p c n", c=4),
                        AF.Copy, scale=0.125),
                        waits=[(P_t, tok_mm), (P_s, P_s.n)], inc=P_s)
                    ps_free[pp] = P_s.n
            tok_q_done_s = P_s.n
            tok_q_done_t = P_t.n
            qstack.close()
            gp.op(lambda q: q.dma_start(
                out=WKB[:].rearrange("p (c f) -> p c f", c=NKC),
                in_=woutT_in.rearrange("(c p) e -> p c e", p=128)),
                waits=[(P_t, tok_q_done_t)], inc=D_g)
            tok_wo = D_g.n
            ASB = [sbt(f"ASB{i}", [128, 2048], BF16) for i in range(3)]
            ATS = [sbt(f"ATS{i}", [128, 1024], BF16) for i in range(2)]
            OTM = [sbt(f"OTM{i}", [128, 1024], BF16) for i in range(2)]
            OFM = [sbt(f"OFM{i}", [128, 1024], BF16) for i in range(2)]

            slot_free_s = [tok_q_done_s] * 4
            pso_free_s = [(P_s, tok_q_done_s), (P_s, tok_q_done_s)]
            asb_free_t = [0, 0, 0]
            bx_free_s = [0, 0, 0]
            rl_free_s = [(P_s, 0), (P_s, 0), (P_s, 0)]
            lp_free_v = [0, 0, 0]
            otm_store_a = [0, 0]
            ofm_free_t = [0, 0]
            gci = 0
            TB = [PS4B, PS7B]
            TBF = [PS4, PS7]
            tb_evac = [(P_s, tok_q_done_s), (P_s, tok_q_done_s)]
            tb_ring = [0]
            ats_av_t = [0, 0]
            k3 = KF[:].rearrange("p (c n) -> p c n", c=NKC)
            v3 = VT[:].rearrange("p (c n) -> p c n", c=NT)

            def stage_S(qt, h, vg):
                nonlocal gci
                ntk = qt + 1
                W = ntk * 128
                hb = (h % 2) * 64
                hc = h // 2
                apar = vg % 3
                chunks = []
                koff = 0
                while koff < W:
                    w = min(512, W - koff)
                    chunks.append((koff, w))
                    koff += w
                nch = len(chunks)
                slot0 = gci
                merged = nch <= 2 and (slot0 % 4) + nch <= 4
                for j, (ko, w) in enumerate(chunks):
                    slot = gci % 4
                    gci += 1
                    sp0 = slot * 512
                    last = (j == nch - 1)
                    te.op(lambda q, ko=ko, w=w, sp0=sp0, hb=hb, hc=hc, qt=qt,
                          last=last: q.matmul(
                              PSB[:, sp0:sp0 + w],
                              x3q[hb:hb + 64, hc, qt * 128:(qt + 1) * 128],
                              k3[hb:hb + 64, hc, ko:ko + w],
                              start=True, stop=not last),
                          waits=[(P_s, slot_free_s[slot])],
                          inc=(None if last else P_t))
                    if last:
                        te.op(lambda q, sp0=sp0, w=w: q.matmul(
                            PSB[:, sp0 + w - 128:sp0 + w],
                            IDN[:], MASKB[:],
                            start=False, stop=True),
                            waits=[(P_g, TOK_MASK)], inc=P_t)
                    tok_smm = P_t.n
                    if not merged:
                        ve.op(lambda q, sp0=sp0, w=w, j=j: q.tensor_reduce(
                            RMX[:, j:j + 1], PSB[:, sp0:sp0 + w], AX.X, ALU.max),
                            waits=[(P_t, tok_smm)],
                            inc=(P_v if j == nch - 1 else None))
                if merged:
                    s0 = (slot0 % 4) * 512
                    ve.op(lambda q, s0=s0, W=W, apar=apar: q.tensor_reduce(
                        BX[:, apar:apar + 1], PSB[:, s0:s0 + W], AX.X, ALU.max,
                        negate=True),
                        waits=[(P_t, tok_smm), (P_v, P_v.n),
                               (P_s, bx_free_s[apar])], inc=P_v)
                else:
                    ve.op(lambda q, nch=nch, apar=apar: q.tensor_reduce(
                        BX[:, apar:apar + 1], RMX[:, 0:nch], AX.X, ALU.max,
                        negate=True),
                        waits=[(P_v, P_v.n), (P_s, bx_free_s[apar])], inc=P_v)
                tok_bx = P_v.n
                if merged:
                    s0 = (slot0 % 4) * 512
                    sc.op(lambda q, s0=s0, W=W, apar=apar: q.activation(
                        ASB[apar][:, 0:W], PSB[:, s0:s0 + W],
                        AF.Exp,
                        bias=BX[:, apar:apar + 1],
                        accum_out=LP[:, apar * 16:apar * 16 + 1]),
                        waits=[(P_v, tok_bx), (P_t, asb_free_t[apar]),
                               (P_v, lp_free_v[apar])], inc=P_s)
                    for j in range(nch):
                        slot_free_s[(slot0 + j) % 4] = P_s.n
                else:
                    for j, (ko, w) in enumerate(chunks):
                        slot = (slot0 + j) % 4
                        sp0 = slot * 512
                        sc.op(lambda q, ko=ko, w=w, sp0=sp0, j=j, apar=apar:
                              q.activation(
                                  ASB[apar][:, ko:ko + w], PSB[:, sp0:sp0 + w],
                                  AF.Exp,
                                  bias=BX[:, apar:apar + 1],
                                  accum_out=LP[:, apar * 16 + j:
                                               apar * 16 + j + 1]),
                              waits=([(P_v, tok_bx), (P_t, asb_free_t[apar]),
                                      (P_v, lp_free_v[apar])] if j == 0
                                     else [(P_v, tok_bx)]), inc=P_s)
                        slot_free_s[slot] = P_s.n
                tok_exp = P_s.n
                bx_free_s[apar] = tok_exp
                return {"qt": qt, "h": h, "ntk": ntk, "nch": nch, "apar": apar,
                        "opar": vg % 2, "tok_exp": tok_exp,
                        "merged": merged}

            def stage_T(c):
                h, ntk, apar, tok_exp = c["h"], c["ntk"], c["apar"], c["tok_exp"]
                nch = c["nch"]
                if c["merged"]:
                    ve.op(lambda q, apar=apar: q.reciprocal(
                        RL[:, apar:apar + 1], LP[:, apar * 16:apar * 16 + 1]),
                        waits=[(P_s, tok_exp), (P_v, P_v.n),
                               rl_free_s[apar]], inc=P_v)
                    lp_free_v[apar] = P_v.n
                else:
                    ve.op(lambda q, nch=nch, apar=apar: q.tensor_reduce(
                        LL[:], LP[:, apar * 16:apar * 16 + nch], AX.X, ALU.add),
                        waits=[(P_s, tok_exp), (P_v, P_v.n)], inc=P_v)
                    lp_free_v[apar] = P_v.n
                    ve.op(lambda q, apar=apar: q.reciprocal(
                        RL[:, apar:apar + 1], LL[:]),
                        waits=[(P_v, P_v.n), rl_free_s[apar]], inc=P_v)
                c["tok_rl"] = P_v.n
                nbat = (ntk + 7) // 8
                bats = []
                for b in range(nbat):
                    b0 = b * 8
                    bw = min(8, ntk - b0)
                    sl = tb_ring[0]
                    tb_ring[0] ^= 1
                    for j2 in range(bw):
                        w0 = []
                        if j2 == 0:
                            w0 = [(P_s, tok_exp), tb_evac[sl],
                                  (P_g, TOK_IDN)]
                        te.op(lambda q, j2=j2, b0=b0, sl=sl, apar=apar:
                              q.transpose(
                                  TB[sl][:, j2 * 128:(j2 + 1) * 128],
                                  ASB[apar][:, (b0 + j2) * 128:
                                            (b0 + j2 + 1) * 128],
                                  IDN[:]),
                              waits=w0, inc=(P_t if j2 == bw - 1 else None))
                    bats.append((b0, bw, sl, P_t.n))
                c["bats"] = bats
                asb_free_t[apar] = P_t.n
                return c

            def stage_C(c):
                h = c["h"]
                nb = []
                for bi, (b0, bw, sl, tok_tp) in enumerate(c["bats"]):
                    if (h + bi) % 2 == 0:
                        sc.op(lambda q, bw=bw, sl=sl: q.activation(
                            ATS[sl][:, 0:bw * 128],
                            TB[sl][:, 0:bw * 128], AF.Copy),
                            waits=[(P_t, tok_tp), (P_t, ats_av_t[sl]),
                                   (P_s, P_s.n)], inc=P_s)
                        ev = (P_s, P_s.n)
                    else:
                        ve.op(lambda q, bw=bw, sl=sl: q.tensor_copy(
                            ATS[sl][:, 0:bw * 128],
                            TB[sl][:, 0:bw * 128]),
                            waits=[(P_t, tok_tp), (P_t, ats_av_t[sl]),
                                   (P_v, P_v.n)], inc=P_v)
                        ev = (P_v, P_v.n)
                    tb_evac[sl] = ev
                    nb.append((b0, bw, sl, ev))
                c["bats"] = nb
                return c

            def stage_AV(c):
                h, ntk, opar = c["h"], c["ntk"], c["opar"]
                pso = PS5 if opar == 0 else PS6
                first = True
                for (b0, bw, sl, ev) in c["bats"]:
                    for j2 in range(bw):
                        w0 = [ev] if j2 == 0 else []
                        if first and j2 == 0:
                            w0.append(pso_free_s[opar])
                        te.op(lambda q, j2=j2, b0=b0, sl=sl, h=h, pso=pso:
                              q.matmul(
                                  pso[:, 0:64],
                                  ATS[sl][:, j2 * 128:(j2 + 1) * 128],
                                  v3[:, b0 + j2, h * 64:(h + 1) * 64],
                                  start=(b0 + j2 == 0),
                                  stop=(b0 + j2 == ntk - 1)),
                              waits=w0,
                              inc=(P_t if j2 == bw - 1 else None))
                    ats_av_t[sl] = P_t.n
                    first = False
                c["tok_av"] = P_t.n
                return c

            def stage_OEV(c):
                h, apar, opar, qt = c["h"], c["apar"], c["opar"], c["qt"]
                pso = PS5 if opar == 0 else PS6
                ve.op(lambda q, h=h, pso=pso, apar=apar, qt=qt: q.tensor_scalar(
                    OTM[qt % 2][:, h * 64:(h + 1) * 64], pso[:, 0:64],
                    RL[:, apar:apar + 1], None, ALU.mult),
                    waits=[(P_t, c["tok_av"]), (P_v, c["tok_rl"]),
                           (D_a, otm_store_a[qt % 2]), (P_v, P_v.n)], inc=P_v)
                pso_free_s[opar] = (P_v, P_v.n)
                rl_free_s[apar] = (P_v, P_v.n)

            def emit_outproj(octx):
                qt, tok_ofm, opar2 = octx
                sl = tb_ring[0]
                tb_ring[0] ^= 1
                for half in range(2):
                    for kc in range(NKC):
                        w0 = []
                        if kc == 0:
                            w0 = [(D_s, tok_ofm), (D_g, tok_wo)]
                            if half == 0:
                                w0.append(tb_evac[sl])
                            else:
                                w0.append((P_v, halftok[0]))
                        te.op(lambda q, kc=kc, half=half, opar2=opar2, sl=sl:
                              q.matmul(
                                  TBF[sl][:, 0:512],
                                  OFM[opar2][:, kc * 128:(kc + 1) * 128],
                                  WKB[:, kc * 1024 + half * 512:
                                      kc * 1024 + (half + 1) * 512],
                                  start=(kc == 0), stop=False),
                              waits=w0)
                    te.op(lambda q, half=half, sl=sl: q.matmul(
                        TBF[sl][:, 0:512], ONES1[:],
                        BOUTB[0:1, half * 512:(half + 1) * 512],
                        start=False, stop=True),
                        waits=[(P_g, TOK_ONES), (D_g, tok_bout)], inc=P_t)
                    tok_op = P_t.n
                    if half == 1:
                        ofm_free_t[opar2] = tok_op
                    xs = X[:, qt * D + half * 512:qt * D + half * 512 + 512]
                    ve.op(lambda q, xs=xs, sl=sl: q.tensor_tensor(
                        xs, TBF[sl][:, 0:512], xs, ALU.add),
                        waits=[(P_t, tok_op), (P_v, P_v.n)], inc=P_v)
                    halftok[0] = P_v.n
                tb_evac[sl] = (P_v, P_v.n)

            halftok = [0]
            pend_ops = []
            vg = 0
            for qp in range(0, NT, 2):
                vlist = []
                for h in range(HEADS):
                    vlist.append((qp, h))
                    vlist.append((qp + 1, h))
                p1, p2, p3, p4 = [], [], [], []
                for vi, (qt, h) in enumerate(vlist):
                    if p4:
                        stage_OEV(p4.pop(0))
                    if p3:
                        p4.append(stage_AV(p3.pop(0)))
                    if p2:
                        p3.append(stage_C(p2.pop(0)))
                    if vi in (4, 10) and pend_ops:
                        emit_outproj(pend_ops.pop(0))
                    sctx = stage_S(qt, h, vg)
                    vg += 1
                    if p1:
                        p2.append(stage_T(p1.pop(0)))
                    p1.append(sctx)
                while p1 or p2 or p3 or p4:
                    if p4:
                        stage_OEV(p4.pop(0))
                    if p3:
                        p4.append(stage_AV(p3.pop(0)))
                    if p2:
                        p3.append(stage_C(p2.pop(0)))
                    if p1:
                        p2.append(stage_T(p1.pop(0)))
                tok_o = P_v.n
                for qt in (qp, qp + 1):
                    sc.op(lambda q, qt=qt: q.dma_start(
                        out=o_scr[qt * 1024:(qt + 1) * 1024, :]
                        .rearrange("(c p) f -> c p f", p=128)
                        .rearrange("c p f -> p c f"),
                        in_=OTM[qt % 2][:].rearrange("p (c f) -> p c f", f=128)),
                        waits=[(P_v, tok_o), (P_s, P_s.n)], inc=D_a)
                    otm_store_a[qt % 2] = D_a.n
                    opar2 = qt % 2
                    sy.op(lambda q, qt=qt, opar2=opar2: q.dma_start(
                        out=OFM[opar2][:, 0:1024],
                        in_=o_scr[qt * 1024:(qt + 1) * 1024, :],
                        transpose=True),
                        waits=[(D_a, otm_store_a[qt % 2]),
                               (P_t, ofm_free_t[opar2])],
                        inc=D_s)
                    pend_ops.append((qt, D_s.n, opar2))
            for octx in pend_ops:
                emit_outproj(octx)
            st_["x_tok_v"] = P_v.n

        def ffn_stack(ps, tag=""):
            def sbt(name, shape, dtype):
                return ps.enter_context(nc.sbuf_tensor(name + tag, shape, dtype))
            return {
                "XQF": sbt("XQF", [128, NKC * 2048], BF16),
                "HQF": [sbt(f"HQF{i}", [128, 2048], BF16) for i in range(2)],
                "WGF": [sbt(f"WGF{i}", [128, NKC * 512], BF16) for i in range(2)],
                "WUF": [sbt(f"WUF{i}", [128, NKC * 512], BF16) for i in range(2)],
                "WDF": sbt("WDF", [128, NOFC * 512], BF16),
                "WRAW": sbt("WRAW", [128, 2048], F32),
                "SGT": [sbt(f"SGT{i}", [128, 512], F32) for i in range(2)],
                "QA": sbt("QA", [128, 2048], F32),
                "QB": sbt("QB", [128, 2048], F32),
                "QM": sbt("QM", [128, 2048], F32),
                "CAND": sbt("CAND", [128, 16], F32),
                "QS": [sbt(f"QS{i}", [128, 64], F32) for i in range(4)],
                "QI": [sbt(f"QI{i}", [128, 64], F32) for i in range(4)],
                "QD": [sbt(f"QD{i}", [128, 64], F32) for i in range(4)],
                "QO": [sbt(f"QO{i}", [128, 1024], BF16) for i in range(4)],
            }

        with ExitStack() as ps1:
            fs = ffn_stack(ps1)
            emit_ffn_sublayer(fs, first=True)
        if debug_stop != "ffn1":
            barrier()
            with ExitStack() as ps2:
                emit_attention(ps2)
            if debug_stop != "attn":
                barrier()
                with ExitStack() as ps3:
                    fs = ffn_stack(ps3, tag="b")
                    emit_ffn_sublayer(fs, first=False)

        if st_.get("x_half0_v"):
            sy.op(lambda q: q.dma_start(
                out=out[0:8 * 128, :].rearrange("(t p) f -> p t f", p=128),
                in_=X[:, 0:8 * D].rearrange("p (t f) -> p t f", t=8)),
                waits=[(P_v, st_["x_half0_v"])], inc=D_s)
            sy.op(lambda q: q.dma_start(
                out=out[8 * 128:S, :].rearrange("(t p) f -> p t f", p=128),
                in_=X[:, 8 * D:NT * D].rearrange("p (t f) -> p t f", t=8)),
                waits=all_counts(), inc=D_s)
        else:
            sy.op(lambda q: q.dma_start(
                out=out.rearrange("(t p) f -> p t f", p=128),
                in_=X[:].rearrange("p (t f) -> p t f", t=NT)),
                waits=all_counts(), inc=D_s)
        fin = D_s.n
        sy.ops.append(lambda q: q.wait_ge(D_s.h, fin))

        with nc.Block() as block:
            @block.sync
            def _(q):
                for f in sy.ops:
                    f(q)

            @block.vector
            def _(q):
                for f in ve.ops:
                    f(q)

            @block.scalar
            def _(q):
                for f in sc.ops:
                    f(q)

            @block.tensor
            def _(q):
                for f in te.ops:
                    f(q)

            @block.gpsimd
            def _(q):
                for f in gp.ops:
                    f(q)

    return nc


def kernel(L0_attn_x, L0_attn_ln_gamma, L0_attn_ln_beta, L0_attn_w_qkv,
           L0_attn_w_out, L0_attn_b_out, L0_ffn_w_gate, L0_ffn_w_up,
           L0_ffn_w_down, _debug_stop=None):
    x = np.ascontiguousarray(np.asarray(L0_attn_x, dtype=np.float32))
    wqkvT = np.ascontiguousarray(np.asarray(L0_attn_w_qkv, dtype=np.float32).T)
    woutT = np.ascontiguousarray(np.asarray(L0_attn_w_out, dtype=np.float32).T)
    bout = np.ascontiguousarray(np.asarray(L0_attn_b_out, dtype=np.float32)[None, :])
    gam = np.ascontiguousarray(
        np.broadcast_to(np.asarray(L0_attn_ln_gamma, dtype=np.float32), (128, D)))
    bet = np.ascontiguousarray(
        np.broadcast_to(np.asarray(L0_attn_ln_beta, dtype=np.float32), (128, D)))
    wg = np.ascontiguousarray(np.asarray(L0_ffn_w_gate, dtype=np.float32))
    wu = np.ascontiguousarray(np.asarray(L0_ffn_w_up, dtype=np.float32))
    wd = np.ascontiguousarray(np.asarray(L0_ffn_w_down, dtype=np.float32))

    nc = build_program(_debug_stop)
    in_maps = []
    for c in range(8):
        in_maps.append({
            "x": np.ascontiguousarray(x[c]),
            "wg": wg, "wu": wu, "wd": wd,
            "wqkvT": wqkvT, "woutT": woutT, "bout": bout,
            "gam": gam, "bet": bet,
        })
    res = run_bass_kernel_spmd(nc, in_maps, core_ids=list(range(8)))
    outs = [np.asarray(res.results[c]["out"], dtype=np.float32) for c in range(8)]
    return np.stack(outs, axis=0)


# version 48
# speedup vs baseline: 1.0671x; 1.0671x over previous
import numpy as np
import ml_dtypes
from contextlib import ExitStack

import concourse.bass as bass
import concourse.mybir as mybir
from concourse.bass_utils import run_bass_kernel_spmd

F32 = mybir.dt.float32
BF16 = mybir.dt.bfloat16
I32 = mybir.dt.int32
AF = mybir.ActivationFunctionType
ALU = mybir.AluOpType
AX = mybir.AxisListType
BF = ml_dtypes.bfloat16

S = 2048
D = 1024
NT = 16
IH = 2816
NKC = 8
HEADS = 16
EPS = 1e-5

B_LO = 0x3F800000
B_HI = 0x40800000
B_MBOFF = 0x0B400000

GATE_CH = [(0, 512), (512, 512), (1024, 512), (1536, 512), (2048, 512), (2560, 256)]
REGW = [0]
for _c in GATE_CH:
    REGW.append(REGW[-1] + 8 * _c[1])
NOFC = IH // 128
NWT = IH // 128


class SemC:
    def __init__(self, handle, step=1):
        self.h = handle
        self.step = step
        self.n = 0


class Eng:
    def __init__(self, name):
        self.name = name
        self.ops = []

    def op(self, fn, waits=(), inc=None):
        tok = None
        if inc is not None:
            inc.n += inc.step
            tok = inc.n
        wl = [(s.h, v) for (s, v) in waits if v and v > 0]
        ih = inc.h if inc is not None else None
        istep = inc.step if inc is not None else 0

        def run(q):
            for h, v in wl:
                q.wait_ge(h, v)
            ins = fn(q)
            if ih is not None:
                ins.then_inc(ih, istep)
        self.ops.append(run)
        return tok


def build_program(debug_stop=None):
    nc = bass.Bass("TRN2", target_bir_lowering=False)

    x_in = nc.dram_tensor("x", [S, D], F32, kind="ExternalInput").ap()
    wg_in = nc.dram_tensor("wg", [IH, D], F32, kind="ExternalInput").ap()
    wu_in = nc.dram_tensor("wu", [IH, D], F32, kind="ExternalInput").ap()
    wd_in = nc.dram_tensor("wd", [D, IH], F32, kind="ExternalInput").ap()
    wqkvT_in = nc.dram_tensor("wqkvT", [D, 3 * D], F32, kind="ExternalInput").ap()
    woutT_in = nc.dram_tensor("woutT", [D, D], F32, kind="ExternalInput").ap()
    bout_in = nc.dram_tensor("bout", [1, D], F32, kind="ExternalInput").ap()
    gam_in = nc.dram_tensor("gam", [128, D], F32, kind="ExternalInput").ap()
    bet_in = nc.dram_tensor("bet", [128, D], F32, kind="ExternalInput").ap()
    out = nc.dram_tensor("out", [S, D], F32, kind="ExternalOutput").ap()

    xq_scr = nc.dram_tensor("xq_scr", [8 * 2048, 128], BF16, kind="Internal").ap()
    hq_scr = nc.dram_tensor("hq_scr", [4 * NOFC * 512, 128], BF16,
                            kind="Internal").ap()
    wgq_scr = nc.dram_tensor("wgq_scr", [8 * IH, 128], BF16, kind="Internal").ap()
    wuq_scr = nc.dram_tensor("wuq_scr", [8 * IH, 128], BF16, kind="Internal").ap()
    wdq_scr = nc.dram_tensor("wdq_scr", [2 * NOFC * 512, 128], BF16,
                            kind="Internal").ap()
    xn2_scr = nc.dram_tensor("xn2_scr", [NKC, S, 128], BF16, kind="Internal").ap()
    o_scr = nc.dram_tensor("o_scr", [NKC * S, 128], BF16, kind="Internal").ap()

    gstack = ExitStack()
    with gstack:
        def gsbt(name, shape, dtype):
            return gstack.enter_context(nc.sbuf_tensor(name, shape, dtype))

        X = gsbt("X", [128, NT * D], F32)
        IDN = gsbt("IDN", [128, 128], BF16)
        MASKB = gsbt("MASKB", [128, 128], BF16)
        ONES1 = gsbt("ONES1", [1, 128], BF16)
        EPS1 = gsbt("EPS1", [128, 1], F32)
        MV = gsbt("MV", [128, 2 * NT], F32)
        LNV = gsbt("LNV", [128, NT], F32)
        RSTD = gsbt("RSTD", [128, NT], F32)
        NEGMU = gsbt("NEGMU", [128, NT], F32)
        NMR = gsbt("NMR", [128, NT], F32)
        STT = gsbt("STT", [128, 12], F32)

        PSB = gstack.enter_context(nc.psum_tensor("PSB", [128, 2048], F32))
        PS4 = gstack.enter_context(nc.psum_tensor("PS4", [128, 512], F32))
        PS5 = gstack.enter_context(nc.psum_tensor("PS5", [128, 512], F32))
        PS6 = gstack.enter_context(nc.psum_tensor("PS6", [128, 512], F32))
        PS7 = gstack.enter_context(nc.psum_tensor("PS7", [128, 512], F32))
        PS4B = PS4[:].bitcast(BF16)
        PS7B = PS7[:].bitcast(BF16)

        P_t = SemC(gstack.enter_context(nc.semaphore("P_t")))
        P_v = SemC(gstack.enter_context(nc.semaphore("P_v")))
        P_s = SemC(gstack.enter_context(nc.semaphore("P_s")))
        P_g = SemC(gstack.enter_context(nc.semaphore("P_g")))
        D_s = SemC(gstack.enter_context(nc.semaphore("D_s")), step=16)
        D_a = SemC(gstack.enter_context(nc.semaphore("D_a")), step=16)
        D_g = SemC(gstack.enter_context(nc.semaphore("D_g")), step=16)

        sy, ve, sc, te, gp = Eng("sy"), Eng("ve"), Eng("sc"), Eng("te"), Eng("gp")
        ENGS = (sy, ve, sc, te, gp)

        st_ = {
            "qb_free_v": [0, 0, 0, 0],
            "qb_store_da": [0, 0, 0, 0],
            "qbp": 0,
            "xqf_free_t": 0,
            "wgf_free_t": [0, 0],
            "wdf_free_t": 0,
            "gps_free_s": [0, 0],
            "ups_free_v": [0, 0],
            "sgt_free_v": [0, 0],
            "ht_free_v": [0, 0],
            "down_free_v": 0,
            "hq_read_ds": 0,
            "u_tok": [0, 0, 0, 0],
            "x_tok_v": 0,
            "wq_tok_u": {},
            "wq_tok_d": {},
        }

        def all_counts():
            return [(P_t, P_t.n), (P_v, P_v.n), (P_s, P_s.n), (P_g, P_g.n),
                    (D_s, D_s.n), (D_a, D_a.n), (D_g, D_g.n)]

        def barrier():
            toks = all_counts()
            for e in ENGS:
                e.op(lambda q: q.nop(), waits=toks)

        sy.op(lambda q: q.dma_start(
            out=X[:].rearrange("p (t f) -> p t f", t=NT),
            in_=x_in.rearrange("(t p) f -> p t f", p=128)), inc=D_s)
        tok_xload = D_s.n

        ve.op(lambda q: q.memset(EPS1[:], EPS), inc=P_v)
        gp.op(lambda q: q.memset(IDN[:], 0.0), inc=P_g)
        gp.op(lambda q: q.affine_select(
            out=IDN[:], in_=IDN[:], compare_op=ALU.not_equal, fill=1.0,
            base=0, pattern=[[-1, 128]], channel_multiplier=1),
            waits=[(P_g, P_g.n)], inc=P_g)
        gp.op(lambda q: q.memset(MASKB[:], 0.0), waits=[(P_g, P_g.n)], inc=P_g)
        gp.op(lambda q: q.affine_select(
            out=MASKB[:], in_=MASKB[:], compare_op=ALU.is_ge, fill=-1e30,
            base=0, pattern=[[-1, 128]], channel_multiplier=1),
            waits=[(P_g, P_g.n)], inc=P_g)
        gp.op(lambda q: q.memset(ONES1[:], 1.0), waits=[(P_g, P_g.n)], inc=P_g)
        TOK_IDN = 2
        TOK_MASK = 4
        TOK_ONES = 5

        def emit_ln_prepass(want_nmr):
            xv = st_["x_tok_v"]
            for t in range(NT):
                c0 = t * D
                ve.op(lambda q, c0=c0: q.bn_stats(STT[:, 0:6], X[:, c0:c0 + 512]),
                      waits=[(D_s, tok_xload), (P_v, xv), (P_v, P_v.n)], inc=P_v)
                ve.op(lambda q, c0=c0: q.bn_stats(STT[:, 6:12], X[:, c0 + 512:c0 + 1024]),
                      waits=[(P_v, P_v.n)], inc=P_v)
                ve.op(lambda q, t=t: q.bn_aggr(MV[:, 2 * t:2 * t + 2], STT[:]),
                      waits=[(P_v, P_v.n)], inc=P_v)
            tok_mv = P_v.n
            mv3 = MV[:].rearrange("p (t two) -> p t two", two=2)
            varv = mv3[:, :, 1:2].squeeze(2)
            muv = mv3[:, :, 0:1].squeeze(2)
            sc.op(lambda q: q.activation(LNV[:], varv, AF.Ln, bias=EPS1[:]),
                  waits=[(P_v, tok_mv), (P_s, P_s.n)], inc=P_s)
            sc.op(lambda q: q.activation(RSTD[:], LNV[:], AF.Exp, scale=-0.5),
                  waits=[(P_s, P_s.n)], inc=P_s)
            tok_rstd = P_s.n
            ve.op(lambda q: q.tensor_scalar(NEGMU[:], muv, -1.0, None, ALU.mult),
                  waits=[(P_v, tok_mv), (P_v, P_v.n)], inc=P_v)
            if want_nmr:
                ve.op(lambda q: q.scalar_tensor_tensor(
                    NMR[:], muv, -1.0, RSTD[:], ALU.mult, ALU.mult),
                    waits=[(P_s, tok_rstd), (P_v, P_v.n)], inc=P_v)
            return tok_rstd, P_v.n

        def quant_steps(qb, src, W, src_waits, extra_ap=None, extra_const=None):
            nb = W // 16
            Qd = st_["Q"]
            QA, QB, QM, CAND = Qd["QA"], Qd["QB"], Qd["QM"], Qd["CAND"]
            Sq, Iq, Dq, Oq = (Qd["QS"][qb], Qd["QI"][qb], Qd["QD"][qb],
                              Qd["QO"][qb])
            off = (qb % 2) * 1024 if W <= 1024 else 0
            A = QA[:, off:off + W]
            Bq = QB[:, off:off + W]
            Mq = QM[:, off:off + W]
            src3 = src.rearrange("p (nb b) -> p nb b", b=16)
            steps = []

            def s_bm(pt):
                return ve.op(lambda q: q.tensor_reduce(
                    Sq[:, 0:nb], src3, AX.X, ALU.max, apply_absolute_value=True),
                    waits=list(src_waits) + [(P_v, st_["qb_free_v"][qb]),
                                             (P_v, pt)], inc=P_v)
            steps.append(s_bm)

            def s_scale(pt):
                return ve.op(lambda q: q.tensor_scalar(
                    Sq[:, 0:nb], Sq[:, 0:nb], 1.0 / 6.0, 1e-12, ALU.mult,
                    ALU.max), waits=[(P_v, pt)], inc=P_v)
            steps.append(s_scale)

            def s_recip(pt):
                return ve.op(lambda q: q.reciprocal(Iq[:, 0:nb], Sq[:, 0:nb]),
                             waits=[(P_v, pt)], inc=P_v)
            steps.append(s_recip)

            if extra_ap is not None or extra_const is not None:
                xop = extra_ap if extra_ap is not None else extra_const

                def s_dd(pt):
                    return ve.op(lambda q: q.tensor_scalar(
                        Dq[:, 0:nb], Sq[:, 0:nb], xop, None, ALU.mult),
                        waits=[(P_v, pt)], inc=P_v)
                steps.append(s_dd)
                dd = Dq
            else:
                dd = Sq
            inv3 = Iq[:, 0:nb].unsqueeze(2).broadcast_to([128, nb, 16])
            u3 = A.rearrange("p (nb b) -> p nb b", b=16)

            def s_u(pt):
                tok = ve.op(lambda q: q.tensor_tensor(u3, src3, inv3, ALU.mult),
                            waits=[(P_v, pt)], inc=P_v)
                st_["u_tok"][qb] = tok
                return tok
            steps.append(s_u)
            ui = A.bitcast(I32)
            mi = Mq.bitcast(I32)
            candv = CAND[:, 0:1].broadcast_to([128, W]).bitcast(I32)

            def s_and(pt):
                return ve.op(lambda q: q.tensor_tensor(
                    mi, ui, candv, ALU.bitwise_and), waits=[(P_v, pt)], inc=P_v)
            steps.append(s_and)

            def s_clamp(pt):
                return ve.op(lambda q: q.tensor_scalar(
                    Mq, Mq, 1.0, 4.0, ALU.max, ALU.min),
                    waits=[(P_v, pt)], inc=P_v)
            steps.append(s_clamp)

            def s_t1(pt):
                return ve.op(lambda q: q.scalar_tensor_tensor(
                    Bq, Mq, 6291456.0, A, ALU.mult, ALU.add),
                    waits=[(P_v, pt)], inc=P_v)
            steps.append(s_t1)

            def s_q(pt):
                return ve.op(lambda q: q.scalar_tensor_tensor(
                    A, Mq, -6291456.0, Bq, ALU.mult, ALU.add),
                    waits=[(P_v, pt)], inc=P_v)
            steps.append(s_q)
            dd3 = dd[:, 0:nb].unsqueeze(2).broadcast_to([128, nb, 16])
            o3 = Oq[:, 0:W].rearrange("p (nb b) -> p nb b", b=16)

            def s_xq(pt):
                return ve.op(lambda q: q.tensor_tensor(o3, u3, dd3, ALU.mult),
                             waits=[(P_v, pt),
                                    (D_a, st_["qb_store_da"][qb])], inc=P_v)
            steps.append(s_xq)
            return steps

        def run_chains(chains):
            toks = [0] * len(chains)
            idx = [0] * len(chains)
            done = 0
            while done < len(chains):
                done = 0
                for ci in range(len(chains)):
                    if idx[ci] < len(chains[ci]):
                        toks[ci] = chains[ci][idx[ci]](toks[ci])
                        idx[ci] += 1
                    else:
                        done += 1
            return toks

        def emit_ffn_sublayer(fs, first):
            XQF, HQF, WGF, WUF, WDF, WRAW, SGT = (
                fs["XQF"], fs["HQF"], fs["WGF"], fs["WUF"], fs["WDF"],
                fs["WRAW"], fs["SGT"])
            st_["Q"] = fs
            QB_, QO_ = fs["QB"], fs["QO"]
            HT = [QB_[:, 0:512], QB_[:, 1024:1536]]
            ve.op(lambda q: q.memset(fs["CAND"][:], float("inf")),
                  waits=[(P_v, P_v.n)], inc=P_v)
            tok_rstd, tok_negmu = emit_ln_prepass(want_nmr=False)

            wpairs = []
            for t in range(NWT):
                wpairs.append([("g", wg_in, wgq_scr, t, 0, D),
                               ("u", wu_in, wuq_scr, t, 0, D)])
            for t in range(D // 128):
                wpairs.append([("d", wd_in, wdq_scr, t, 0, 1024),
                               ("d", wd_in, wdq_scr, t, 1024, 1024)])
                wpairs.append([("d", wd_in, wdq_scr, t, 2048, 768)])
            wcur = [0]
            wraw_free = [0, 0, 0, 0]

            loadq = []

            def preload(idx):
                if idx >= len(wpairs):
                    return
                pair = wpairs[idx]
                toks = []
                for e, (kind, src_hbm, dst_scr, t, off, W) in enumerate(pair):
                    sc.op(lambda q, src_hbm=src_hbm, t=t, off=off, W=W,
                          e=e: q.dma_start(
                              out=WRAW[:, e * 1024:e * 1024 + W],
                              in_=src_hbm[t * 128:(t + 1) * 128,
                                          off:off + W]),
                          waits=[(P_v, wraw_free[e])], inc=D_a)
                    toks.append(D_a.n)
                toks = [toks[-1]] * len(toks)
                loadq.append(toks)

            def emit_wpairs(n):
                for _ in range(n):
                    if wcur[0] >= len(wpairs):
                        return
                    if not loadq:
                        preload(wcur[0])
                    pair = wpairs[wcur[0]]
                    base = (wcur[0] % 2) * 2
                    wcur[0] += 1
                    loads = loadq.pop(0)
                    chains = []
                    for e, (kind, src_hbm, dst_scr, t, off, W) in enumerate(pair):
                        qb = base + e
                        chains.append(quant_steps(
                            qb, WRAW[:, e * 1024:e * 1024 + W], W,
                            [(D_a, loads[e])],
                            extra_const=(0.5 if kind == "d" else None)))
                    toks = run_chains(chains)
                    for e in range(len(pair)):
                        wraw_free[e] = st_["u_tok"][base + e]
                    preload(wcur[0])
                    for e, (kind, src_hbm, dst_scr, t, off, W) in enumerate(pair):
                        qb = base + e
                        st_["qb_free_v"][qb] = toks[e]
                        nsl = W // 128
                        c0 = off // 128
                        if kind == "d":
                            dst = wdq_scr[(t // 4) * 11264:
                                          (t // 4 + 1) * 11264, :] \
                                .rearrange("(c p) f -> c p f", p=512) \
                                [c0:c0 + nsl,
                                 (t % 4) * 128:(t % 4) * 128 + 128, :]
                        else:
                            ci_w = min(t // 4, len(GATE_CH) - 1)
                            coff_w, cw_w = GATE_CH[ci_w]
                            dst = dst_scr[REGW[ci_w]:REGW[ci_w] + 8 * cw_w, :] \
                                .rearrange("(c p) f -> c p f", p=cw_w) \
                                [0:8, t * 128 - coff_w:t * 128 - coff_w + 128, :]
                        sc.op(lambda q, dst=dst, qb=qb, W=W: q.dma_start(
                                  out=dst.rearrange("c p f -> p c f"),
                                  in_=QO_[qb][:, 0:W]
                                  .rearrange("p (c f) -> p c f", f=128)),
                              waits=[(P_v, toks[e])], inc=D_a)
                        st_["qb_store_da"][qb] = D_a.n
                        if kind == "u":
                            st_["wq_tok_u"][t] = D_a.n
                        elif kind == "d":
                            st_["wq_tok_d"][t] = D_a.n

            if first:
                emit_wpairs(4)

            xqf_tok = [0] * NT
            for t0p in range(0, NT, 2):
                base = (t0p // 2 % 2) * 2
                chains = []
                srcs = []
                for e in range(2):
                    t = t0p + e
                    qb = base + e
                    off = e * 1024
                    src = QB_[:, off:off + D]
                    sc.op(lambda q, t=t, src=src: q.activation(
                        src, X[:, t * D:(t + 1) * D], AF.Identity,
                        bias=NEGMU[:, t:t + 1]),
                        waits=[(P_v, tok_negmu), (P_v, st_["qb_free_v"][qb]),
                               (P_s, P_s.n)], inc=P_s)
                    srcs.append(P_s.n)
                    chains.append(quant_steps(qb, src, D, [(P_s, srcs[e])],
                                              extra_ap=RSTD[:, t:t + 1]))
                toks = run_chains(chains)
                for e in range(2):
                    t = t0p + e
                    qb = base + e
                    st_["qb_free_v"][qb] = toks[e]
                    sc.op(lambda q, t=t, qb=qb: q.dma_start(
                        out=xq_scr[(t // 2) * 2048:(t // 2) * 2048 + 2048, :]
                        .rearrange("(c p) f -> c p f", p=256)
                        [0:8, (t % 2) * 128:(t % 2) * 128 + 128, :]
                        .rearrange("c p f -> p c f"),
                        in_=QO_[qb][:, 0:D].rearrange("p (c f) -> p c f", f=128)),
                        waits=[(P_v, toks[e])], inc=D_a)
                    st_["qb_store_da"][qb] = D_a.n
                tok_pair_store = D_a.n
                pr = t0p // 2
                sy.op(lambda q, pr=pr: q.dma_start(
                    out=XQF[:, pr * 2048:(pr + 1) * 2048],
                    in_=xq_scr[pr * 2048:(pr + 1) * 2048, :],
                    transpose=True),
                    waits=[(D_a, tok_pair_store), (P_t, st_["xqf_free_t"])],
                    inc=D_s)
                xqf_tok[t0p] = xqf_tok[t0p + 1] = D_s.n

            hq_done = [[0] * len(GATE_CH) for _ in range(2)]
            banksPS47 = [PS4[:, 0:512], PS5[:, 0:512], PS6[:, 0:512],
                         PS7[:, 0:512]]
            banksPSB = [PSB[:, 0:512], PSB[:, 512:1024], PSB[:, 1024:1536],
                        PSB[:, 1536:2048]]
            hqf_free = [0, 0]

            def emit_wdf(half, free_tok):
                wq_wait = st_["wq_tok_d"].get(D // 128 - 1, 0) if first else 0
                sy.op(lambda q, half=half: q.dma_start(
                    out=WDF[:, 0:NOFC * 512],
                    in_=wdq_scr[half * 11264:(half + 1) * 11264, :],
                    transpose=True),
                    waits=[(P_t, free_tok), (D_a, wq_wait)],
                    inc=D_s)
                return D_s.n

            def emit_down_group(sg, half, quad, wdf_tok, banks, bank_waits):
                tbase = sg * 8 + quad * 4
                tg = tbase // 4
                hq_tok = hq_done[sg][len(GATE_CH) - 1]
                pan_tok = [0] * 6
                def load_panel(p):
                    if p > 5:
                        return
                    w = (min(4 * p + 4, NOFC) - 4 * p) * 512
                    sy.op(lambda q, p=p, tg=tg, w=w: q.dma_start(
                        out=HQF[p % 2][:, 0:w],
                        in_=hq_scr[tg * 11264 + p * 2048:
                                   tg * 11264 + p * 2048 + w, :],
                        transpose=True),
                        waits=[(D_a, hq_tok), (P_t, hqf_free[p % 2])], inc=D_s)
                    pan_tok[p] = D_s.n
                load_panel(0)
                load_panel(1)
                for ofc in range(NOFC):
                    p = ofc // 4
                    col = (ofc % 4) * 512
                    for j in range(4):
                        w0 = [(D_s, pan_tok[p])] \
                            if (ofc % 4 == 0 and j == 0) else []
                        if ofc == 0 and j == 0:
                            w0 += [(D_s, wdf_tok)] + list(bank_waits)
                        te.op(lambda q, j=j, p=p, col=col, ofc=ofc, banks=banks:
                              q.matmul(
                                  banks[j],
                                  HQF[p % 2][:, col + j * 128:
                                              col + (j + 1) * 128],
                                  WDF[:, ofc * 512:(ofc + 1) * 512],
                                  start=(ofc == 0), stop=(ofc == NOFC - 1)),
                              waits=w0, inc=(P_t if (j == 3) else None))
                    if ofc % 4 == 3 or ofc == NOFC - 1:
                        hqf_free[p % 2] = P_t.n
                        load_panel(p + 2)
                mm_tok = P_t.n

                def evac():
                    for j in range(4):
                        t = tbase + j
                        xs = X[:, t * D + half * 512:t * D + half * 512 + 512]
                        ve.op(lambda q, xs=xs, banks=banks, j=j: q.tensor_tensor(
                            xs, banks[j], xs, ALU.add),
                            waits=[(P_t, mm_tok), (P_v, P_v.n)], inc=P_v)
                    return P_v.n
                return evac, mm_tok

            ds_ = {"ps47": 0, "wdf": 0}
            ev_g = [None] * 8
            mm_g = [0] * 8
            def emit_chunk_loads(sg, ci):
                coff, cw = GATE_CH[ci]
                wpar = ci % 2
                wq_wait = st_["wq_tok_u"].get((coff + cw) // 128 - 1, 0) \
                    if first else 0
                sy.op(lambda q, ci=ci, cw=cw, wpar=wpar: q.dma_start(
                    out=WGF[wpar][:, 0:8 * cw],
                    in_=wgq_scr[REGW[ci]:REGW[ci] + 8 * cw, :],
                    transpose=True),
                    waits=[(P_t, st_["wgf_free_t"][wpar]), (D_a, wq_wait)],
                    inc=D_s)
                sy.op(lambda q, ci=ci, cw=cw, wpar=wpar: q.dma_start(
                    out=WUF[wpar][:, 0:8 * cw],
                    in_=wuq_scr[REGW[ci]:REGW[ci] + 8 * cw, :],
                    transpose=True),
                    inc=D_s)
                return D_s.n

            tok_w_next = [0]
            for sg in range(2):
                for ci, (coff, cw) in enumerate(GATE_CH):
                    wpar = ci % 2
                    if sg == 0 and ci == 0:
                        tok_w = emit_chunk_loads(0, 0)
                    else:
                        tok_w = tok_w_next[0]
                    if ci < len(GATE_CH) - 1:
                        if first:
                            emit_wpairs(max(0, min(4 * ci + 8, NWT) - wcur[0]))
                        tok_w_next[0] = emit_chunk_loads(sg, ci + 1)
                    elif sg == 0:
                        tok_w_next[0] = emit_chunk_loads(1, 0)
                    tok_w_a = 0
                    for t0p in range(sg * 8, sg * 8 + 8, 2):
                        base = (t0p // 2 % 2) * 2
                        hsrc = []
                        for e in range(2):
                            t = t0p + e
                            ph = e
                            g_ps = PSB[:, ph * 512:ph * 512 + cw]
                            u_ps = PSB[:, 1024 + ph * 512:1024 + ph * 512 + cw]
                            for kc in range(NKC):
                                w0 = []
                                if kc == 0:
                                    w0 = [(D_s, tok_w), (D_a, tok_w_a),
                                          (D_s, xqf_tok[t]),
                                          (P_s, st_["gps_free_s"][ph]),
                                          (P_v, st_["ups_free_v"][ph])]
                                te.op(lambda q, kc=kc, t=t, cw=cw, g_ps=g_ps,
                                      wpar=wpar: q.matmul(
                                          g_ps,
                                          XQF[:, (t // 2) * 2048 + kc * 256 +
                                              (t % 2) * 128:
                                              (t // 2) * 2048 + kc * 256 +
                                              (t % 2) * 128 + 128],
                                          WGF[wpar][:, kc * cw:kc * cw + cw],
                                          start=(kc == 0), stop=(kc == NKC - 1)),
                                      waits=w0, inc=(P_t if kc == NKC - 1 else None))
                            for kc in range(NKC):
                                te.op(lambda q, kc=kc, t=t, cw=cw, u_ps=u_ps,
                                      wpar=wpar: q.matmul(
                                          u_ps,
                                          XQF[:, (t // 2) * 2048 + kc * 256 +
                                              (t % 2) * 128:
                                              (t // 2) * 2048 + kc * 256 +
                                              (t % 2) * 128 + 128],
                                          WUF[wpar][:, kc * cw:kc * cw + cw],
                                          start=(kc == 0), stop=(kc == NKC - 1)),
                                      inc=(P_t if kc == NKC - 1 else None))
                            tok_mm = P_t.n
                            if t == sg * 8 + 7:
                                st_["wgf_free_t"][wpar] = tok_mm
                            sc.op(lambda q, e=e, cw=cw, g_ps=g_ps: q.activation(
                                SGT[e][:, 0:cw], g_ps, AF.Silu),
                                waits=[(P_t, tok_mm), (P_v, st_["sgt_free_v"][e]),
                                       (P_s, P_s.n)], inc=P_s)
                            st_["gps_free_s"][ph] = P_s.n
                            tok_sg = P_s.n
                            ve.op(lambda q, e=e, cw=cw, u_ps=u_ps: q.tensor_tensor(
                                HT[e][:, 0:cw], SGT[e][:, 0:cw], u_ps, ALU.mult),
                                waits=[(P_s, tok_sg), (P_v, st_["ht_free_v"][e]),
                                       (P_v, P_v.n)], inc=P_v)
                            st_["ups_free_v"][ph] = P_v.n
                            st_["sgt_free_v"][e] = P_v.n
                            hsrc.append(P_v.n)
                        chains = [quant_steps(base + e, HT[e][:, 0:cw], cw,
                                              [(P_v, hsrc[e])]) for e in range(2)]
                        toks = run_chains(chains)
                        nsl = cw // 128
                        for e in range(2):
                            t = t0p + e
                            qb = base + e
                            st_["ht_free_v"][e] = toks[e]
                            st_["qb_free_v"][qb] = toks[e]
                            sc.op(lambda q, qb=qb, coff=coff, nsl=nsl, t=t, cw=cw:
                                  q.dma_start(
                                      out=hq_scr[(t // 4) * 11264:
                                                 (t // 4 + 1) * 11264, :]
                                      .rearrange("(c p) f -> c p f", p=512)
                                      [coff // 128:coff // 128 + nsl,
                                       (t % 4) * 128:(t % 4) * 128 + 128, :]
                                      .rearrange("c p f -> p c f"),
                                      in_=QO_[qb][:, 0:cw]
                                      .rearrange("p (c f) -> p c f", f=128)),
                                  waits=[(P_v, toks[e]), (D_s, st_["hq_read_ds"])],
                                  inc=D_a)
                            st_["qb_store_da"][qb] = D_a.n
                    if first:
                        emit_wpairs(4)
                    hq_done[sg][ci] = D_a.n
                    if sg == 1 and ci == 0:
                        ds_["wdf"] = emit_wdf(0, st_["wdf_free_t"])
                        ev_g[0], mm_g[0] = emit_down_group(
                            0, 0, 0, ds_["wdf"], banksPS47,
                            [(P_v, ds_["ps47"])])
                    elif sg == 1 and ci == 1:
                        ds_["ps47"] = ev_g[0]()
                        ev_g[1], mm_g[1] = emit_down_group(
                            0, 0, 1, ds_["wdf"], banksPS47,
                            [(P_v, ds_["ps47"])])
                    elif sg == 1 and ci == 2:
                        ds_["ps47"] = ev_g[1]()
                        ds_["wdf"] = emit_wdf(1, mm_g[1])
                        ev_g[2], mm_g[2] = emit_down_group(
                            0, 1, 0, ds_["wdf"], banksPS47,
                            [(P_v, ds_["ps47"])])
                    elif sg == 1 and ci == 3:
                        ds_["ps47"] = ev_g[2]()
                        ev_g[3], mm_g[3] = emit_down_group(
                            0, 1, 1, ds_["wdf"], banksPS47,
                            [(P_v, ds_["ps47"])])
                    elif sg == 1 and ci == 4:
                        ds_["ps47"] = ev_g[3]()
                        st_["x_half0_v"] = ds_["ps47"]
                if sg == 0 and first:
                    emit_wpairs(len(wpairs))
            st_["xqf_free_t"] = P_t.n

            psb_waits = [(P_s, st_["gps_free_s"][0]), (P_s, st_["gps_free_s"][1]),
                         (P_v, st_["ups_free_v"][0]), (P_v, st_["ups_free_v"][1])]
            tok_wdf = emit_wdf(0, mm_g[3])
            ev_g[4], mm_g[4] = emit_down_group(1, 0, 0, tok_wdf, banksPS47,
                                               [(P_v, ds_["ps47"])])
            ev_g[5], mm_g[5] = emit_down_group(1, 0, 1, tok_wdf, banksPSB,
                                               psb_waits)
            ps47f = ev_g[4]()
            tok_wdf = emit_wdf(1, mm_g[5])
            ev_g[6], mm_g[6] = emit_down_group(1, 1, 0, tok_wdf, banksPS47,
                                               [(P_v, ps47f)])
            psbf = ev_g[5]()
            ev_g[7], mm_g[7] = emit_down_group(1, 1, 1, tok_wdf, banksPSB,
                                               [(P_v, psbf)])
            ev_g[6]()
            ev_g[7]()
            st_["wdf_free_t"] = mm_g[7]
            st_["hq_read_ds"] = D_s.n
            st_["x_tok_v"] = P_v.n

        def emit_attention(ps):
            st_["x_half0_v"] = 0
            astack = ps

            def sbt(name, shape, dtype):
                return astack.enter_context(nc.sbuf_tensor(name, shape, dtype))
            WKB = sbt("WKB", [128, NKC * 1024], BF16)
            XN2F = sbt("XN2F", [128, NKC * 2048], BF16)
            KF = sbt("KF", [128, NKC * 2048], BF16)
            VT = sbt("VT", [128, NT * 1024], BF16)
            BOUTB = sbt("BOUTB", [1, D], BF16)
            R2 = sbt("R2", [128, 4], F32)
            RMX = sbt("RMX", [128, 16], F32)
            LP = sbt("LP", [128, 48], F32)
            BX = sbt("BX", [128, 3], F32)
            RL = sbt("RL", [128, 3], F32)
            NMX = sbt("NMX", [128, 1], F32)
            LL = sbt("LL", [128, 1], F32)
            inner = ExitStack()
            GAM = inner.enter_context(nc.sbuf_tensor("GAM", [128, D], F32))
            BET = inner.enter_context(nc.sbuf_tensor("BET", [128, D], F32))
            XN1 = [inner.enter_context(nc.sbuf_tensor(f"XN1{i}", [128, D], F32))
                   for i in range(2)]
            XN1B = [inner.enter_context(nc.sbuf_tensor(f"XN1B{i}", [128, D], F32))
                    for i in range(2)]
            XN2B = [inner.enter_context(nc.sbuf_tensor("XN2B0", [128, D], BF16))]
            XN2B.append(XN2B[0])

            tok_rstd1, tok_nmr1 = emit_ln_prepass(want_nmr=True)
            sc.op(lambda q: q.dma_start(out=GAM[:], in_=gam_in), inc=D_a)
            sc.op(lambda q: q.dma_start(out=BET[:], in_=bet_in), inc=D_a)
            tok_gb = D_a.n
            gp.op(lambda q: q.dma_start(out=BOUTB[:], in_=bout_in),
                  waits=[(P_g, P_g.n)], inc=D_g)
            tok_bout = D_g.n
            wqkv3 = wqkvT_in.rearrange("(c p) e -> p c e", p=128)
            gp.op(lambda q: q.dma_start(
                out=WKB[:].rearrange("p (c f) -> p c f", c=NKC),
                in_=wqkv3[:, :, D:2 * D]), inc=D_g)
            tok_wk = D_g.n

            QRING = [PSB[:, 0:512], PSB[:, 512:1024], PSB[:, 1024:1536],
                     PSB[:, 1536:2048], PS5[:, 0:512], PS6[:, 0:512]]
            ps_free = [0] * 6
            xn1_free_s = [0, 0]
            xn2b_free_a = [0, 0]
            xn2_store_h = [0, 0]
            xn2_store_q = [0, 0, 0, 0]
            xn2f_qtok = [0, 0, 0, 0]
            pend_kevac = []

            def flush_kevac():
                for (kf, tg4, ps, pp, tok_kmm) in pend_kevac:
                    sc.op(lambda q, kf=kf, tg4=tg4, ps=ps: q.activation(
                        KF[:, kf * 2048 + tg4 * 512:
                           kf * 2048 + (tg4 + 1) * 512],
                        ps, AF.Copy),
                        waits=[(P_t, tok_kmm)], inc=P_s)
                    ps_free[pp] = P_s.n
                del pend_kevac[:]
            for t in range(NT):
                par = t % 2
                sc.op(lambda q, t=t, par=par: q.activation(
                    XN1[par][:], X[:, t * D:(t + 1) * D], AF.Identity,
                    scale=RSTD[:, t:t + 1], bias=NMR[:, t:t + 1]),
                    waits=[(P_s, tok_rstd1), (P_v, tok_nmr1),
                           (P_v, xn1_free_s[par]), (P_s, P_s.n)], inc=P_s)
                tok_x1 = P_s.n
                ve.op(lambda q, par=par: q.bn_stats(STT[:, 0:6], XN1[par][:, 0:512]),
                      waits=[(P_s, tok_x1), (P_v, P_v.n)], inc=P_v)
                ve.op(lambda q, par=par: q.bn_stats(
                    STT[:, 6:12], XN1[par][:, 512:1024]),
                    waits=[(P_v, P_v.n)], inc=P_v)
                ve.op(lambda q, par=par: q.bn_aggr(
                    R2[:, 2 * par:2 * par + 2], STT[:]),
                    waits=[(P_v, P_v.n)], inc=P_v)
                tok_st2 = P_v.n
                sc.op(lambda q, par=par: q.activation(
                    LNV[:, par:par + 1], R2[:, 2 * par + 1:2 * par + 2], AF.Ln,
                    bias=EPS1[:]),
                    waits=[(P_v, tok_st2), (P_s, P_s.n)], inc=P_s)
                sc.op(lambda q, par=par: q.activation(
                    BX[:, par:par + 1], LNV[:, par:par + 1], AF.Exp, scale=-0.5),
                    waits=[(P_s, P_s.n)], inc=P_s)
                tok_r2 = P_s.n
                ve.op(lambda q, par=par: q.scalar_tensor_tensor(
                    NMX[:], R2[:, 2 * par:2 * par + 1], -1.0,
                    BX[:, par:par + 1], ALU.mult, ALU.mult),
                    waits=[(P_s, tok_r2), (P_v, P_v.n)], inc=P_v)
                tok_nm2 = P_v.n
                sc.op(lambda q, par=par: q.activation(
                    XN1B[par][:], XN1[par][:], AF.Identity,
                    scale=BX[:, par:par + 1], bias=NMX[:]),
                    waits=[(P_v, tok_nm2), (P_s, P_s.n)], inc=P_s)
                tok_x2a = P_s.n
                xn1_free_s[par] = tok_x2a
                ve.op(lambda q, par=par: q.tensor_tensor(
                    XN1[par][:], XN1B[par][:], GAM[:], ALU.mult),
                    waits=[(P_s, tok_x2a), (D_a, tok_gb), (P_v, P_v.n)], inc=P_v)
                ve.op(lambda q, par=par: q.tensor_tensor(
                    XN2B[par][:], XN1[par][:], BET[:], ALU.add),
                    waits=[(P_v, P_v.n), (D_g, xn2b_free_a[par])], inc=P_v)
                tok_x2 = P_v.n
                gp.op(lambda q, t=t, par=par: q.dma_start(
                    out=xn2_scr[:, t * 128:(t + 1) * 128, :]
                    .rearrange("c p f -> p c f"),
                    in_=XN2B[par][:].rearrange("p (c f) -> p c f", f=128)),
                    waits=[(P_v, tok_x2)], inc=D_g)
                xn2b_free_a[par] = D_g.n
                if t % 4 == 3:
                    tg4 = t // 4
                    xn2_store_q[tg4] = D_g.n
                    flush_kevac()
                    for kc in range(NKC):
                        sy.op(lambda q, kc=kc, tg4=tg4: q.dma_start(
                            out=XN2F[:, kc * 2048 + tg4 * 512:
                                     kc * 2048 + tg4 * 512 + 512],
                            in_=xn2_scr[kc, tg4 * 512:(tg4 + 1) * 512, :],
                            transpose=True),
                            waits=[(D_g, xn2_store_q[tg4])], inc=D_s)
                    xn2f_qtok[tg4] = D_s.n
                    kbatch = []
                    for kf in range(NKC):
                        pp = (tg4 * NKC + kf) % 6
                        ps = QRING[pp]
                        for kc in range(NKC):
                            w0 = []
                            if kc == 0:
                                w0 = [(D_g, tok_wk),
                                      (D_s, xn2f_qtok[tg4]),
                                      (P_s, ps_free[pp])]
                            te.op(lambda q, kc=kc, kf=kf, tg4=tg4, ps=ps:
                                  q.matmul(
                                      ps,
                                      WKB[:, kc * 1024 + kf * 128:
                                          kc * 1024 + (kf + 1) * 128],
                                      XN2F[:, kc * 2048 + tg4 * 512:
                                           kc * 2048 + (tg4 + 1) * 512],
                                      start=(kc == 0), stop=(kc == NKC - 1)),
                                  waits=w0,
                                  inc=(P_t if kc == NKC - 1 else None))
                        kbatch.append((kf, tg4, ps, pp, P_t.n))
                        if kf >= 2:
                            pend_kevac.append(kbatch.pop(0))
                            e = pend_kevac.pop(0) if len(pend_kevac) > 1 \
                                else pend_kevac.pop()
                            (ekf, etg, eps, epp, etok) = e
                            sc.op(lambda q, ekf=ekf, etg=etg, eps=eps:
                                  q.activation(
                                      KF[:, ekf * 2048 + etg * 512:
                                         ekf * 2048 + (etg + 1) * 512],
                                      eps, AF.Copy),
                                  waits=[(P_t, etok)], inc=P_s)
                            ps_free[epp] = P_s.n
                    pend_kevac.extend(kbatch)
            xn2_store_h[1] = D_g.n
            inner.close()
            qstack = ExitStack()
            WTMP = qstack.enter_context(
                nc.sbuf_tensor("WTMP", [128, NKC * 1024], BF16))
            gp.op(lambda q: q.dma_start(
                out=WTMP[:].rearrange("p (c f) -> p c f", c=NKC),
                in_=wqkv3[:, :, 2 * D:3 * D]),
                waits=[(D_g, xn2_store_h[1])], inc=D_g)
            tok_wv = D_g.n

            flush_kevac()
            tok_k_done_t = P_t.n
            gp.op(lambda q: q.dma_start(
                out=WKB[:].rearrange("p (c f) -> p c f", c=NKC),
                in_=wqkv3[:, :, 0:D]),
                waits=[(P_t, tok_k_done_t)], inc=D_g)
            tok_wq = D_g.n
            for t in range(NT):
                for vc in range(2):
                    pp = (t * 2 + vc) % 6
                    ps = QRING[pp]
                    for kc in range(NKC):
                        w0 = []
                        if kc == 0:
                            w0 = [(D_g, tok_wv), (P_s, ps_free[pp])]
                        te.op(lambda q, kc=kc, t=t, vc=vc, ps=ps: q.matmul(
                            ps,
                            XN2F[:, kc * 2048 + t * 128:kc * 2048 + (t + 1) * 128],
                            WTMP[:, kc * 1024 + vc * 512:kc * 1024 + (vc + 1) * 512],
                            start=(kc == 0), stop=(kc == NKC - 1)),
                            waits=w0, inc=(P_t if kc == NKC - 1 else None))
                    tok_mm = P_t.n
                    sc.op(lambda q, t=t, vc=vc, ps=ps: q.activation(
                        VT[:, t * 1024 + vc * 512:t * 1024 + (vc + 1) * 512],
                        ps, AF.Copy),
                        waits=[(P_t, tok_mm)], inc=P_s)
                    ps_free[pp] = P_s.n
            tok_v_done_t = P_t.n
            x3q = XN2F[:].rearrange("p (c n) -> p c n", c=NKC)
            for t in range(NT):
                for qf in range(NKC):
                    pp = (t * 2 + qf // 4) % 6
                    ps = QRING[pp]
                    qq = qf % 4
                    for kc in range(NKC):
                        w0 = []
                        if kc == 0 and qq == 0:
                            w0 = [(D_g, tok_wq), (P_s, ps_free[pp])]
                        te.op(lambda q, kc=kc, t=t, qf=qf, ps=ps, qq=qq: q.matmul(
                            ps[:, qq * 128:(qq + 1) * 128],
                            WKB[:, kc * 1024 + qf * 128:kc * 1024 + (qf + 1) * 128],
                            XN2F[:, kc * 2048 + t * 128:kc * 2048 + (t + 1) * 128],
                            start=(kc == 0), stop=(kc == NKC - 1)),
                            waits=w0, inc=(P_t if kc == NKC - 1 else None))
                tok_mm = P_t.n
                for half in range(2):
                    pp = (t * 2 + half) % 6
                    ps = QRING[pp]
                    sc.op(lambda q, t=t, half=half, ps=ps: q.activation(
                        x3q[:, 4 * half:4 * half + 4, t * 128:(t + 1) * 128],
                        ps.rearrange("p (c n) -> p c n", c=4),
                        AF.Copy, scale=0.125),
                        waits=[(P_t, tok_mm), (P_s, P_s.n)], inc=P_s)
                    ps_free[pp] = P_s.n
            tok_q_done_s = P_s.n
            tok_q_done_t = P_t.n
            qstack.close()
            gp.op(lambda q: q.dma_start(
                out=WKB[:].rearrange("p (c f) -> p c f", c=NKC),
                in_=woutT_in.rearrange("(c p) e -> p c e", p=128)),
                waits=[(P_t, tok_q_done_t)], inc=D_g)
            tok_wo = D_g.n
            ASB = [sbt(f"ASB{i}", [128, 2048], BF16) for i in range(3)]
            ATS = [sbt(f"ATS{i}", [128, 1024], BF16) for i in range(2)]
            OTM = [sbt(f"OTM{i}", [128, 1024], BF16) for i in range(2)]
            OFM = [sbt(f"OFM{i}", [128, 1024], BF16) for i in range(2)]

            slot_free_s = [tok_q_done_s] * 4
            pso_free_s = [(P_s, tok_q_done_s), (P_s, tok_q_done_s)]
            asb_free_t = [0, 0, 0]
            bx_free_s = [0, 0, 0]
            rl_free_s = [(P_s, 0), (P_s, 0), (P_s, 0)]
            lp_free_v = [0, 0, 0]
            otm_store_a = [0, 0]
            ofm_free_t = [0, 0]
            gci = 0
            TB = [PS4B, PS7B]
            TBF = [PS4, PS7]
            tb_evac = [(P_s, tok_q_done_s), (P_s, tok_q_done_s)]
            tb_ring = [0]
            ats_av_t = [0, 0]
            k3 = KF[:].rearrange("p (c n) -> p c n", c=NKC)
            v3 = VT[:].rearrange("p (c n) -> p c n", c=NT)

            def stage_S(qt, h, vg):
                nonlocal gci
                ntk = qt + 1
                W = ntk * 128
                hb = (h % 2) * 64
                hc = h // 2
                apar = vg % 3
                chunks = []
                koff = 0
                while koff < W:
                    w = min(512, W - koff)
                    chunks.append((koff, w))
                    koff += w
                nch = len(chunks)
                slot0 = gci
                merged = nch <= 2 and (slot0 % 4) + nch <= 4
                for j, (ko, w) in enumerate(chunks):
                    slot = gci % 4
                    gci += 1
                    sp0 = slot * 512
                    last = (j == nch - 1)
                    te.op(lambda q, ko=ko, w=w, sp0=sp0, hb=hb, hc=hc, qt=qt,
                          last=last: q.matmul(
                              PSB[:, sp0:sp0 + w],
                              x3q[hb:hb + 64, hc, qt * 128:(qt + 1) * 128],
                              k3[hb:hb + 64, hc, ko:ko + w],
                              start=True, stop=not last),
                          waits=[(P_s, slot_free_s[slot])],
                          inc=(None if last else P_t))
                    if last:
                        te.op(lambda q, sp0=sp0, w=w: q.matmul(
                            PSB[:, sp0 + w - 128:sp0 + w],
                            IDN[:], MASKB[:],
                            start=False, stop=True),
                            waits=[(P_g, TOK_MASK)], inc=P_t)
                    tok_smm = P_t.n
                    if not merged:
                        ve.op(lambda q, sp0=sp0, w=w, j=j: q.tensor_reduce(
                            RMX[:, j:j + 1], PSB[:, sp0:sp0 + w], AX.X, ALU.max),
                            waits=[(P_t, tok_smm)],
                            inc=(P_v if j == nch - 1 else None))
                if merged:
                    s0 = (slot0 % 4) * 512
                    ve.op(lambda q, s0=s0, W=W, apar=apar: q.tensor_reduce(
                        BX[:, apar:apar + 1], PSB[:, s0:s0 + W], AX.X, ALU.max,
                        negate=True),
                        waits=[(P_t, tok_smm), (P_v, P_v.n),
                               (P_s, bx_free_s[apar])], inc=P_v)
                else:
                    ve.op(lambda q, nch=nch, apar=apar: q.tensor_reduce(
                        BX[:, apar:apar + 1], RMX[:, 0:nch], AX.X, ALU.max,
                        negate=True),
                        waits=[(P_v, P_v.n), (P_s, bx_free_s[apar])], inc=P_v)
                tok_bx = P_v.n
                if merged:
                    s0 = (slot0 % 4) * 512
                    sc.op(lambda q, s0=s0, W=W, apar=apar: q.activation(
                        ASB[apar][:, 0:W], PSB[:, s0:s0 + W],
                        AF.Exp,
                        bias=BX[:, apar:apar + 1],
                        accum_out=LP[:, apar * 16:apar * 16 + 1]),
                        waits=[(P_v, tok_bx), (P_t, asb_free_t[apar]),
                               (P_v, lp_free_v[apar])], inc=P_s)
                    for j in range(nch):
                        slot_free_s[(slot0 + j) % 4] = P_s.n
                else:
                    for j, (ko, w) in enumerate(chunks):
                        slot = (slot0 + j) % 4
                        sp0 = slot * 512
                        sc.op(lambda q, ko=ko, w=w, sp0=sp0, j=j, apar=apar:
                              q.activation(
                                  ASB[apar][:, ko:ko + w], PSB[:, sp0:sp0 + w],
                                  AF.Exp,
                                  bias=BX[:, apar:apar + 1],
                                  accum_out=LP[:, apar * 16 + j:
                                               apar * 16 + j + 1]),
                              waits=([(P_v, tok_bx), (P_t, asb_free_t[apar]),
                                      (P_v, lp_free_v[apar])] if j == 0
                                     else [(P_v, tok_bx)]), inc=P_s)
                        slot_free_s[slot] = P_s.n
                tok_exp = P_s.n
                bx_free_s[apar] = tok_exp
                return {"qt": qt, "h": h, "ntk": ntk, "nch": nch, "apar": apar,
                        "opar": vg % 2, "tok_exp": tok_exp,
                        "merged": merged}

            def stage_T(c):
                h, ntk, apar, tok_exp = c["h"], c["ntk"], c["apar"], c["tok_exp"]
                nch = c["nch"]
                if c["merged"]:
                    ve.op(lambda q, apar=apar: q.reciprocal(
                        RL[:, apar:apar + 1], LP[:, apar * 16:apar * 16 + 1]),
                        waits=[(P_s, tok_exp), (P_v, P_v.n),
                               rl_free_s[apar]], inc=P_v)
                    lp_free_v[apar] = P_v.n
                else:
                    ve.op(lambda q, nch=nch, apar=apar: q.tensor_reduce(
                        LL[:], LP[:, apar * 16:apar * 16 + nch], AX.X, ALU.add),
                        waits=[(P_s, tok_exp), (P_v, P_v.n)], inc=P_v)
                    lp_free_v[apar] = P_v.n
                    ve.op(lambda q, apar=apar: q.reciprocal(
                        RL[:, apar:apar + 1], LL[:]),
                        waits=[(P_v, P_v.n), rl_free_s[apar]], inc=P_v)
                c["tok_rl"] = P_v.n
                nbat = (ntk + 7) // 8
                bats = []
                for b in range(nbat):
                    b0 = b * 8
                    bw = min(8, ntk - b0)
                    sl = tb_ring[0]
                    tb_ring[0] ^= 1
                    for j2 in range(bw):
                        w0 = []
                        if j2 == 0:
                            w0 = [(P_s, tok_exp), tb_evac[sl],
                                  (P_g, TOK_IDN)]
                        te.op(lambda q, j2=j2, b0=b0, sl=sl, apar=apar:
                              q.transpose(
                                  TB[sl][:, j2 * 128:(j2 + 1) * 128],
                                  ASB[apar][:, (b0 + j2) * 128:
                                            (b0 + j2 + 1) * 128],
                                  IDN[:]),
                              waits=w0, inc=(P_t if j2 == bw - 1 else None))
                    bats.append((b0, bw, sl, P_t.n))
                c["bats"] = bats
                asb_free_t[apar] = P_t.n
                return c

            def stage_C(c):
                h = c["h"]
                nb = []
                for bi, (b0, bw, sl, tok_tp) in enumerate(c["bats"]):
                    if (h + bi) % 2 == 0:
                        sc.op(lambda q, bw=bw, sl=sl: q.activation(
                            ATS[sl][:, 0:bw * 128],
                            TB[sl][:, 0:bw * 128], AF.Copy),
                            waits=[(P_t, tok_tp), (P_t, ats_av_t[sl]),
                                   (P_s, P_s.n)], inc=P_s)
                        ev = (P_s, P_s.n)
                    else:
                        ve.op(lambda q, bw=bw, sl=sl: q.tensor_copy(
                            ATS[sl][:, 0:bw * 128],
                            TB[sl][:, 0:bw * 128]),
                            waits=[(P_t, tok_tp), (P_t, ats_av_t[sl]),
                                   (P_v, P_v.n)], inc=P_v)
                        ev = (P_v, P_v.n)
                    tb_evac[sl] = ev
                    nb.append((b0, bw, sl, ev))
                c["bats"] = nb
                return c

            def stage_AV(c):
                h, ntk, opar = c["h"], c["ntk"], c["opar"]
                pso = PS5 if opar == 0 else PS6
                first = True
                for (b0, bw, sl, ev) in c["bats"]:
                    for j2 in range(bw):
                        w0 = [ev] if j2 == 0 else []
                        if first and j2 == 0:
                            w0.append(pso_free_s[opar])
                        te.op(lambda q, j2=j2, b0=b0, sl=sl, h=h, pso=pso:
                              q.matmul(
                                  pso[:, 0:64],
                                  ATS[sl][:, j2 * 128:(j2 + 1) * 128],
                                  v3[:, b0 + j2, h * 64:(h + 1) * 64],
                                  start=(b0 + j2 == 0),
                                  stop=(b0 + j2 == ntk - 1)),
                              waits=w0,
                              inc=(P_t if j2 == bw - 1 else None))
                    ats_av_t[sl] = P_t.n
                    first = False
                c["tok_av"] = P_t.n
                return c

            def stage_OEV(c):
                h, apar, opar, qt = c["h"], c["apar"], c["opar"], c["qt"]
                pso = PS5 if opar == 0 else PS6
                ve.op(lambda q, h=h, pso=pso, apar=apar, qt=qt: q.tensor_scalar(
                    OTM[qt % 2][:, h * 64:(h + 1) * 64], pso[:, 0:64],
                    RL[:, apar:apar + 1], None, ALU.mult),
                    waits=[(P_t, c["tok_av"]), (P_v, c["tok_rl"]),
                           (D_a, otm_store_a[qt % 2]), (P_v, P_v.n)], inc=P_v)
                pso_free_s[opar] = (P_v, P_v.n)
                rl_free_s[apar] = (P_v, P_v.n)

            def emit_outproj(octx):
                qt, tok_ofm, opar2 = octx
                sl = tb_ring[0]
                tb_ring[0] ^= 1
                for half in range(2):
                    for kc in range(NKC):
                        w0 = []
                        if kc == 0:
                            w0 = [(D_s, tok_ofm), (D_g, tok_wo)]
                            if half == 0:
                                w0.append(tb_evac[sl])
                            else:
                                w0.append((P_v, halftok[0]))
                        te.op(lambda q, kc=kc, half=half, opar2=opar2, sl=sl:
                              q.matmul(
                                  TBF[sl][:, 0:512],
                                  OFM[opar2][:, kc * 128:(kc + 1) * 128],
                                  WKB[:, kc * 1024 + half * 512:
                                      kc * 1024 + (half + 1) * 512],
                                  start=(kc == 0), stop=False),
                              waits=w0)
                    te.op(lambda q, half=half, sl=sl: q.matmul(
                        TBF[sl][:, 0:512], ONES1[:],
                        BOUTB[0:1, half * 512:(half + 1) * 512],
                        start=False, stop=True),
                        waits=[(P_g, TOK_ONES), (D_g, tok_bout)], inc=P_t)
                    tok_op = P_t.n
                    if half == 1:
                        ofm_free_t[opar2] = tok_op
                    xs = X[:, qt * D + half * 512:qt * D + half * 512 + 512]
                    ve.op(lambda q, xs=xs, sl=sl: q.tensor_tensor(
                        xs, TBF[sl][:, 0:512], xs, ALU.add),
                        waits=[(P_t, tok_op), (P_v, P_v.n)], inc=P_v)
                    halftok[0] = P_v.n
                tb_evac[sl] = (P_v, P_v.n)

            halftok = [0]
            pend_ops = []
            vg = 0
            for qp in range(0, NT, 2):
                vlist = []
                for h in range(HEADS):
                    vlist.append((qp, h))
                    vlist.append((qp + 1, h))
                p1, p2, p3, p4 = [], [], [], []
                for vi, (qt, h) in enumerate(vlist):
                    if p4:
                        stage_OEV(p4.pop(0))
                    if p3:
                        p4.append(stage_AV(p3.pop(0)))
                    if p2:
                        p3.append(stage_C(p2.pop(0)))
                    if vi in (4, 10) and pend_ops:
                        emit_outproj(pend_ops.pop(0))
                    sctx = stage_S(qt, h, vg)
                    vg += 1
                    if p1:
                        p2.append(stage_T(p1.pop(0)))
                    p1.append(sctx)
                while p1 or p2 or p3 or p4:
                    if p4:
                        stage_OEV(p4.pop(0))
                    if p3:
                        p4.append(stage_AV(p3.pop(0)))
                    if p2:
                        p3.append(stage_C(p2.pop(0)))
                    if p1:
                        p2.append(stage_T(p1.pop(0)))
                tok_o = P_v.n
                for qt in (qp, qp + 1):
                    sc.op(lambda q, qt=qt: q.dma_start(
                        out=o_scr[qt * 1024:(qt + 1) * 1024, :]
                        .rearrange("(c p) f -> c p f", p=128)
                        .rearrange("c p f -> p c f"),
                        in_=OTM[qt % 2][:].rearrange("p (c f) -> p c f", f=128)),
                        waits=[(P_v, tok_o), (P_s, P_s.n)], inc=D_a)
                    otm_store_a[qt % 2] = D_a.n
                    opar2 = qt % 2
                    sy.op(lambda q, qt=qt, opar2=opar2: q.dma_start(
                        out=OFM[opar2][:, 0:1024],
                        in_=o_scr[qt * 1024:(qt + 1) * 1024, :],
                        transpose=True),
                        waits=[(D_a, otm_store_a[qt % 2]),
                               (P_t, ofm_free_t[opar2])],
                        inc=D_s)
                    pend_ops.append((qt, D_s.n, opar2))
            for octx in pend_ops:
                emit_outproj(octx)
            st_["x_tok_v"] = P_v.n

        def ffn_stack(ps, tag=""):
            def sbt(name, shape, dtype):
                return ps.enter_context(nc.sbuf_tensor(name + tag, shape, dtype))
            return {
                "XQF": sbt("XQF", [128, NKC * 2048], BF16),
                "HQF": [sbt(f"HQF{i}", [128, 2048], BF16) for i in range(2)],
                "WGF": [sbt(f"WGF{i}", [128, NKC * 512], BF16) for i in range(2)],
                "WUF": [sbt(f"WUF{i}", [128, NKC * 512], BF16) for i in range(2)],
                "WDF": sbt("WDF", [128, NOFC * 512], BF16),
                "WRAW": sbt("WRAW", [128, 2048], F32),
                "SGT": [sbt(f"SGT{i}", [128, 512], F32) for i in range(2)],
                "QA": sbt("QA", [128, 2048], F32),
                "QB": sbt("QB", [128, 2048], F32),
                "QM": sbt("QM", [128, 2048], F32),
                "CAND": sbt("CAND", [128, 16], F32),
                "QS": [sbt(f"QS{i}", [128, 64], F32) for i in range(4)],
                "QI": [sbt(f"QI{i}", [128, 64], F32) for i in range(4)],
                "QD": [sbt(f"QD{i}", [128, 64], F32) for i in range(4)],
                "QO": [sbt(f"QO{i}", [128, 1024], BF16) for i in range(4)],
            }

        with ExitStack() as ps1:
            fs = ffn_stack(ps1)
            emit_ffn_sublayer(fs, first=True)
        if debug_stop != "ffn1":
            barrier()
            with ExitStack() as ps2:
                emit_attention(ps2)
            if debug_stop != "attn":
                barrier()
                with ExitStack() as ps3:
                    fs = ffn_stack(ps3, tag="b")
                    emit_ffn_sublayer(fs, first=False)

        if st_.get("x_half0_v"):
            sy.op(lambda q: q.dma_start(
                out=out[0:8 * 128, :].rearrange("(t p) f -> p t f", p=128),
                in_=X[:, 0:8 * D].rearrange("p (t f) -> p t f", t=8)),
                waits=[(P_v, st_["x_half0_v"])], inc=D_s)
            sy.op(lambda q: q.dma_start(
                out=out[8 * 128:S, :].rearrange("(t p) f -> p t f", p=128),
                in_=X[:, 8 * D:NT * D].rearrange("p (t f) -> p t f", t=8)),
                waits=all_counts(), inc=D_s)
        else:
            sy.op(lambda q: q.dma_start(
                out=out.rearrange("(t p) f -> p t f", p=128),
                in_=X[:].rearrange("p (t f) -> p t f", t=NT)),
                waits=all_counts(), inc=D_s)
        fin = D_s.n
        sy.ops.append(lambda q: q.wait_ge(D_s.h, fin))

        with nc.Block() as block:
            @block.sync
            def _(q):
                for f in sy.ops:
                    f(q)

            @block.vector
            def _(q):
                for f in ve.ops:
                    f(q)

            @block.scalar
            def _(q):
                for f in sc.ops:
                    f(q)

            @block.tensor
            def _(q):
                for f in te.ops:
                    f(q)

            @block.gpsimd
            def _(q):
                for f in gp.ops:
                    f(q)

    return nc


def kernel(L0_attn_x, L0_attn_ln_gamma, L0_attn_ln_beta, L0_attn_w_qkv,
           L0_attn_w_out, L0_attn_b_out, L0_ffn_w_gate, L0_ffn_w_up,
           L0_ffn_w_down, _debug_stop=None):
    x = np.ascontiguousarray(np.asarray(L0_attn_x, dtype=np.float32))
    wqkvT = np.ascontiguousarray(np.asarray(L0_attn_w_qkv, dtype=np.float32).T)
    woutT = np.ascontiguousarray(np.asarray(L0_attn_w_out, dtype=np.float32).T)
    bout = np.ascontiguousarray(np.asarray(L0_attn_b_out, dtype=np.float32)[None, :])
    gam = np.ascontiguousarray(
        np.broadcast_to(np.asarray(L0_attn_ln_gamma, dtype=np.float32), (128, D)))
    bet = np.ascontiguousarray(
        np.broadcast_to(np.asarray(L0_attn_ln_beta, dtype=np.float32), (128, D)))
    wg = np.ascontiguousarray(np.asarray(L0_ffn_w_gate, dtype=np.float32))
    wu = np.ascontiguousarray(np.asarray(L0_ffn_w_up, dtype=np.float32))
    wd = np.ascontiguousarray(np.asarray(L0_ffn_w_down, dtype=np.float32))

    nc = build_program(_debug_stop)
    in_maps = []
    for c in range(8):
        in_maps.append({
            "x": np.ascontiguousarray(x[c]),
            "wg": wg, "wu": wu, "wd": wd,
            "wqkvT": wqkvT, "woutT": woutT, "bout": bout,
            "gam": gam, "bet": bet,
        })
    res = run_bass_kernel_spmd(nc, in_maps, core_ids=list(range(8)))
    outs = [np.asarray(res.results[c]["out"], dtype=np.float32) for c in range(8)]
    return np.stack(outs, axis=0)


# version 49
# speedup vs baseline: 1.0866x; 1.0182x over previous
import numpy as np
import ml_dtypes
from contextlib import ExitStack

import concourse.bass as bass
import concourse.mybir as mybir
from concourse.bass_utils import run_bass_kernel_spmd

F32 = mybir.dt.float32
BF16 = mybir.dt.bfloat16
I32 = mybir.dt.int32
AF = mybir.ActivationFunctionType
ALU = mybir.AluOpType
AX = mybir.AxisListType
BF = ml_dtypes.bfloat16

S = 2048
D = 1024
NT = 16
IH = 2816
NKC = 8
HEADS = 16
EPS = 1e-5

B_LO = 0x3F800000
B_HI = 0x40800000
B_MBOFF = 0x0B400000

GATE_CH = [(0, 512), (512, 512), (1024, 512), (1536, 512), (2048, 512), (2560, 256)]
REGW = [0]
for _c in GATE_CH:
    REGW.append(REGW[-1] + 8 * _c[1])
NOFC = IH // 128
NWT = IH // 128


class SemC:
    def __init__(self, handle, step=1):
        self.h = handle
        self.step = step
        self.n = 0


class Eng:
    def __init__(self, name):
        self.name = name
        self.ops = []

    def op(self, fn, waits=(), inc=None):
        tok = None
        if inc is not None:
            inc.n += inc.step
            tok = inc.n
        wl = [(s.h, v) for (s, v) in waits if v and v > 0]
        ih = inc.h if inc is not None else None
        istep = inc.step if inc is not None else 0

        def run(q):
            for h, v in wl:
                q.wait_ge(h, v)
            ins = fn(q)
            if ih is not None:
                ins.then_inc(ih, istep)
        self.ops.append(run)
        return tok


def build_program(debug_stop=None):
    nc = bass.Bass("TRN2", target_bir_lowering=False)

    x_in = nc.dram_tensor("x", [S, D], F32, kind="ExternalInput").ap()
    wg_in = nc.dram_tensor("wg", [IH, D], F32, kind="ExternalInput").ap()
    wu_in = nc.dram_tensor("wu", [IH, D], F32, kind="ExternalInput").ap()
    wd_in = nc.dram_tensor("wd", [D, IH], F32, kind="ExternalInput").ap()
    wqkvT_in = nc.dram_tensor("wqkvT", [D, 3 * D], F32, kind="ExternalInput").ap()
    woutT_in = nc.dram_tensor("woutT", [D, D], F32, kind="ExternalInput").ap()
    bout_in = nc.dram_tensor("bout", [1, D], F32, kind="ExternalInput").ap()
    gam_in = nc.dram_tensor("gam", [128, D], F32, kind="ExternalInput").ap()
    bet_in = nc.dram_tensor("bet", [128, D], F32, kind="ExternalInput").ap()
    out = nc.dram_tensor("out", [S, D], F32, kind="ExternalOutput").ap()

    xq_scr = nc.dram_tensor("xq_scr", [8 * 2048, 128], BF16, kind="Internal").ap()
    hq_scr = nc.dram_tensor("hq_scr", [4 * NOFC * 512, 128], BF16,
                            kind="Internal").ap()
    wgq_scr = nc.dram_tensor("wgq_scr", [8 * IH, 128], BF16, kind="Internal").ap()
    wuq_scr = nc.dram_tensor("wuq_scr", [8 * IH, 128], BF16, kind="Internal").ap()
    wdq_scr = nc.dram_tensor("wdq_scr", [2 * NOFC * 512, 128], BF16,
                            kind="Internal").ap()
    xn2_scr = nc.dram_tensor("xn2_scr", [NKC, S, 128], BF16, kind="Internal").ap()
    o_scr = nc.dram_tensor("o_scr", [NKC * S, 128], BF16, kind="Internal").ap()

    gstack = ExitStack()
    with gstack:
        def gsbt(name, shape, dtype):
            return gstack.enter_context(nc.sbuf_tensor(name, shape, dtype))

        X = gsbt("X", [128, NT * D], F32)
        IDN = gsbt("IDN", [128, 128], BF16)
        MASKB = gsbt("MASKB", [128, 128], BF16)
        ONES1 = gsbt("ONES1", [1, 128], BF16)
        EPS1 = gsbt("EPS1", [128, 1], F32)
        MV = gsbt("MV", [128, 2 * NT], F32)
        LNV = gsbt("LNV", [128, NT], F32)
        RSTD = gsbt("RSTD", [128, NT], F32)
        NEGMU = gsbt("NEGMU", [128, NT], F32)
        NMR = gsbt("NMR", [128, NT], F32)
        STT = gsbt("STT", [128, 12], F32)

        PSB = gstack.enter_context(nc.psum_tensor("PSB", [128, 2048], F32))
        PS4 = gstack.enter_context(nc.psum_tensor("PS4", [128, 512], F32))
        PS5 = gstack.enter_context(nc.psum_tensor("PS5", [128, 512], F32))
        PS6 = gstack.enter_context(nc.psum_tensor("PS6", [128, 512], F32))
        PS7 = gstack.enter_context(nc.psum_tensor("PS7", [128, 512], F32))
        PS4B = PS4[:].bitcast(BF16)
        PS7B = PS7[:].bitcast(BF16)

        P_t = SemC(gstack.enter_context(nc.semaphore("P_t")))
        P_v = SemC(gstack.enter_context(nc.semaphore("P_v")))
        P_s = SemC(gstack.enter_context(nc.semaphore("P_s")))
        P_g = SemC(gstack.enter_context(nc.semaphore("P_g")))
        D_s = SemC(gstack.enter_context(nc.semaphore("D_s")), step=16)
        D_a = SemC(gstack.enter_context(nc.semaphore("D_a")), step=16)
        D_g = SemC(gstack.enter_context(nc.semaphore("D_g")), step=16)

        sy, ve, sc, te, gp = Eng("sy"), Eng("ve"), Eng("sc"), Eng("te"), Eng("gp")
        ENGS = (sy, ve, sc, te, gp)

        st_ = {
            "qb_free_v": [0, 0, 0, 0],
            "qb_store_da": [0, 0, 0, 0],
            "qbp": 0,
            "xqf_free_t": 0,
            "wgf_free_t": [0, 0],
            "wdf_free_t": 0,
            "gps_free_s": [0, 0],
            "ups_free_v": [0, 0],
            "sgt_free_v": [0, 0],
            "ht_free_v": [0, 0],
            "down_free_v": 0,
            "hq_read_ds": 0,
            "u_tok": [0, 0, 0, 0],
            "x_tok_v": 0,
            "wq_tok_u": {},
            "wq_tok_d": {},
        }

        def all_counts():
            return [(P_t, P_t.n), (P_v, P_v.n), (P_s, P_s.n), (P_g, P_g.n),
                    (D_s, D_s.n), (D_a, D_a.n), (D_g, D_g.n)]

        def barrier():
            toks = all_counts()
            for e in ENGS:
                e.op(lambda q: q.nop(), waits=toks)

        sy.op(lambda q: q.dma_start(
            out=X[:].rearrange("p (t f) -> p t f", t=NT),
            in_=x_in.rearrange("(t p) f -> p t f", p=128)), inc=D_s)
        tok_xload = D_s.n

        ve.op(lambda q: q.memset(EPS1[:], EPS), inc=P_v)
        gp.op(lambda q: q.memset(IDN[:], 0.0), inc=P_g)
        gp.op(lambda q: q.affine_select(
            out=IDN[:], in_=IDN[:], compare_op=ALU.not_equal, fill=1.0,
            base=0, pattern=[[-1, 128]], channel_multiplier=1),
            waits=[(P_g, P_g.n)], inc=P_g)
        gp.op(lambda q: q.memset(MASKB[:], 0.0), waits=[(P_g, P_g.n)], inc=P_g)
        gp.op(lambda q: q.affine_select(
            out=MASKB[:], in_=MASKB[:], compare_op=ALU.is_ge, fill=-1e30,
            base=0, pattern=[[-1, 128]], channel_multiplier=1),
            waits=[(P_g, P_g.n)], inc=P_g)
        gp.op(lambda q: q.memset(ONES1[:], 1.0), waits=[(P_g, P_g.n)], inc=P_g)
        TOK_IDN = 2
        TOK_MASK = 4
        TOK_ONES = 5

        def emit_ln_prepass(want_nmr):
            xv = st_["x_tok_v"]
            for t in range(NT):
                c0 = t * D
                ve.op(lambda q, c0=c0: q.bn_stats(STT[:, 0:6], X[:, c0:c0 + 512]),
                      waits=[(D_s, tok_xload), (P_v, xv), (P_v, P_v.n)], inc=P_v)
                ve.op(lambda q, c0=c0: q.bn_stats(STT[:, 6:12], X[:, c0 + 512:c0 + 1024]),
                      waits=[(P_v, P_v.n)], inc=P_v)
                ve.op(lambda q, t=t: q.bn_aggr(MV[:, 2 * t:2 * t + 2], STT[:]),
                      waits=[(P_v, P_v.n)], inc=P_v)
            tok_mv = P_v.n
            mv3 = MV[:].rearrange("p (t two) -> p t two", two=2)
            varv = mv3[:, :, 1:2].squeeze(2)
            muv = mv3[:, :, 0:1].squeeze(2)
            sc.op(lambda q: q.activation(LNV[:], varv, AF.Ln, bias=EPS1[:]),
                  waits=[(P_v, tok_mv), (P_s, P_s.n)], inc=P_s)
            sc.op(lambda q: q.activation(RSTD[:], LNV[:], AF.Exp, scale=-0.5),
                  waits=[(P_s, P_s.n)], inc=P_s)
            tok_rstd = P_s.n
            ve.op(lambda q: q.tensor_scalar(NEGMU[:], muv, -1.0, None, ALU.mult),
                  waits=[(P_v, tok_mv), (P_v, P_v.n)], inc=P_v)
            if want_nmr:
                ve.op(lambda q: q.scalar_tensor_tensor(
                    NMR[:], muv, -1.0, RSTD[:], ALU.mult, ALU.mult),
                    waits=[(P_s, tok_rstd), (P_v, P_v.n)], inc=P_v)
            return tok_rstd, P_v.n

        def quant_steps(qb, src, W, src_waits, extra_ap=None, extra_const=None):
            nb = W // 16
            Qd = st_["Q"]
            QA, QB, QM, CAND = Qd["QA"], Qd["QB"], Qd["QM"], Qd["CAND"]
            Sq, Iq, Dq, Oq = (Qd["QS"][qb], Qd["QI"][qb], Qd["QD"][qb],
                              Qd["QO"][qb])
            off = (qb % 2) * 1024 if W <= 1024 else 0
            A = QA[:, off:off + W]
            Bq = QB[:, off:off + W]
            Mq = QM[:, off:off + W]
            src3 = src.rearrange("p (nb b) -> p nb b", b=16)
            steps = []

            def s_bm(pt):
                return ve.op(lambda q: q.tensor_reduce(
                    Sq[:, 0:nb], src3, AX.X, ALU.max, apply_absolute_value=True),
                    waits=list(src_waits) + [(P_v, st_["qb_free_v"][qb]),
                                             (P_v, pt)], inc=P_v)
            steps.append(s_bm)

            def s_scale(pt):
                return ve.op(lambda q: q.tensor_scalar(
                    Sq[:, 0:nb], Sq[:, 0:nb], 1.0 / 6.0, 1e-12, ALU.mult,
                    ALU.max), waits=[(P_v, pt)], inc=P_v)
            steps.append(s_scale)

            def s_recip(pt):
                return ve.op(lambda q: q.reciprocal(Iq[:, 0:nb], Sq[:, 0:nb]),
                             waits=[(P_v, pt)], inc=P_v)
            steps.append(s_recip)

            if extra_ap is not None or extra_const is not None:
                xop = extra_ap if extra_ap is not None else extra_const

                def s_dd(pt):
                    return ve.op(lambda q: q.tensor_scalar(
                        Dq[:, 0:nb], Sq[:, 0:nb], xop, None, ALU.mult),
                        waits=[(P_v, pt)], inc=P_v)
                steps.append(s_dd)
                dd = Dq
            else:
                dd = Sq
            inv3 = Iq[:, 0:nb].unsqueeze(2).broadcast_to([128, nb, 16])
            u3 = A.rearrange("p (nb b) -> p nb b", b=16)

            def s_u(pt):
                tok = ve.op(lambda q: q.tensor_tensor(u3, src3, inv3, ALU.mult),
                            waits=[(P_v, pt)], inc=P_v)
                st_["u_tok"][qb] = tok
                return tok
            steps.append(s_u)
            ui = A.bitcast(I32)
            mi = Mq.bitcast(I32)
            candv = CAND[:, 0:1].broadcast_to([128, W]).bitcast(I32)

            def s_and(pt):
                return ve.op(lambda q: q.tensor_tensor(
                    mi, ui, candv, ALU.bitwise_and), waits=[(P_v, pt)], inc=P_v)
            steps.append(s_and)

            def s_clamp(pt):
                return ve.op(lambda q: q.tensor_scalar(
                    Mq, Mq, 1.0, 4.0, ALU.max, ALU.min),
                    waits=[(P_v, pt)], inc=P_v)
            steps.append(s_clamp)

            def s_t1(pt):
                return ve.op(lambda q: q.scalar_tensor_tensor(
                    Bq, Mq, 6291456.0, A, ALU.mult, ALU.add),
                    waits=[(P_v, pt)], inc=P_v)
            steps.append(s_t1)

            def s_q(pt):
                return ve.op(lambda q: q.scalar_tensor_tensor(
                    A, Mq, -6291456.0, Bq, ALU.mult, ALU.add),
                    waits=[(P_v, pt)], inc=P_v)
            steps.append(s_q)
            dd3 = dd[:, 0:nb].unsqueeze(2).broadcast_to([128, nb, 16])
            o3 = Oq[:, 0:W].rearrange("p (nb b) -> p nb b", b=16)

            def s_xq(pt):
                return ve.op(lambda q: q.tensor_tensor(o3, u3, dd3, ALU.mult),
                             waits=[(P_v, pt),
                                    (D_a, st_["qb_store_da"][qb])], inc=P_v)
            steps.append(s_xq)
            return steps

        def run_chains(chains):
            toks = [0] * len(chains)
            idx = [0] * len(chains)
            done = 0
            while done < len(chains):
                done = 0
                for ci in range(len(chains)):
                    if idx[ci] < len(chains[ci]):
                        toks[ci] = chains[ci][idx[ci]](toks[ci])
                        idx[ci] += 1
                    else:
                        done += 1
            return toks

        def emit_ffn_sublayer(fs, first):
            XQF, HQF, WGF, WUF, WDF, WRAW, SGT = (
                fs["XQF"], fs["HQF"], fs["WGF"], fs["WUF"], fs["WDF"],
                fs["WRAW"], fs["SGT"])
            st_["Q"] = fs
            QB_, QO_ = fs["QB"], fs["QO"]
            HT = [QB_[:, 0:512], QB_[:, 1024:1536]]
            ve.op(lambda q: q.memset(fs["CAND"][:], float("inf")),
                  waits=[(P_v, P_v.n)], inc=P_v)
            tok_rstd, tok_negmu = emit_ln_prepass(want_nmr=False)

            wpairs = []
            for t in range(NWT):
                wpairs.append([("g", wg_in, wgq_scr, t, 0, D),
                               ("u", wu_in, wuq_scr, t, 0, D)])
            for t in range(D // 128):
                wpairs.append([("d", wd_in, wdq_scr, t, 0, 1024),
                               ("d", wd_in, wdq_scr, t, 1024, 1024)])
                wpairs.append([("d", wd_in, wdq_scr, t, 2048, 768)])
            wcur = [0]
            wraw_free = [0, 0, 0, 0]

            loadq = []

            def preload(idx):
                if idx >= len(wpairs):
                    return
                pair = wpairs[idx]
                toks = []
                for e, (kind, src_hbm, dst_scr, t, off, W) in enumerate(pair):
                    sc.op(lambda q, src_hbm=src_hbm, t=t, off=off, W=W,
                          e=e: q.dma_start(
                              out=WRAW[:, e * 1024:e * 1024 + W],
                              in_=src_hbm[t * 128:(t + 1) * 128,
                                          off:off + W]),
                          waits=[(P_v, wraw_free[e])], inc=D_a)
                    toks.append(D_a.n)
                toks = [toks[-1]] * len(toks)
                loadq.append(toks)

            def emit_wpairs(n):
                for _ in range(n):
                    if wcur[0] >= len(wpairs):
                        return
                    if not loadq:
                        preload(wcur[0])
                    pair = wpairs[wcur[0]]
                    base = (wcur[0] % 2) * 2
                    wcur[0] += 1
                    loads = loadq.pop(0)
                    chains = []
                    for e, (kind, src_hbm, dst_scr, t, off, W) in enumerate(pair):
                        qb = base + e
                        chains.append(quant_steps(
                            qb, WRAW[:, e * 1024:e * 1024 + W], W,
                            [(D_a, loads[e])],
                            extra_const=(0.5 if kind == "d" else None)))
                    toks = run_chains(chains)
                    for e in range(len(pair)):
                        wraw_free[e] = st_["u_tok"][base + e]
                    preload(wcur[0])
                    for e, (kind, src_hbm, dst_scr, t, off, W) in enumerate(pair):
                        qb = base + e
                        st_["qb_free_v"][qb] = toks[e]
                        nsl = W // 128
                        c0 = off // 128
                        if kind == "d":
                            dst = wdq_scr[(t // 4) * 11264:
                                          (t // 4 + 1) * 11264, :] \
                                .rearrange("(c p) f -> c p f", p=512) \
                                [c0:c0 + nsl,
                                 (t % 4) * 128:(t % 4) * 128 + 128, :]
                        else:
                            ci_w = min(t // 4, len(GATE_CH) - 1)
                            coff_w, cw_w = GATE_CH[ci_w]
                            dst = dst_scr[REGW[ci_w]:REGW[ci_w] + 8 * cw_w, :] \
                                .rearrange("(c p) f -> c p f", p=cw_w) \
                                [0:8, t * 128 - coff_w:t * 128 - coff_w + 128, :]
                        sc.op(lambda q, dst=dst, qb=qb, W=W: q.dma_start(
                                  out=dst.rearrange("c p f -> p c f"),
                                  in_=QO_[qb][:, 0:W]
                                  .rearrange("p (c f) -> p c f", f=128)),
                              waits=[(P_v, toks[e])], inc=D_a)
                        st_["qb_store_da"][qb] = D_a.n
                        if kind == "u":
                            st_["wq_tok_u"][t] = D_a.n
                        elif kind == "d":
                            st_["wq_tok_d"][t] = D_a.n

            if first:
                emit_wpairs(4)

            xqf_tok = [0] * NT
            for t0p in range(0, NT, 2):
                base = (t0p // 2 % 2) * 2
                chains = []
                srcs = []
                for e in range(2):
                    t = t0p + e
                    qb = base + e
                    off = e * 1024
                    src = QB_[:, off:off + D]
                    sc.op(lambda q, t=t, src=src: q.activation(
                        src, X[:, t * D:(t + 1) * D], AF.Identity,
                        bias=NEGMU[:, t:t + 1]),
                        waits=[(P_v, tok_negmu), (P_v, st_["qb_free_v"][qb]),
                               (P_s, P_s.n)], inc=P_s)
                    srcs.append(P_s.n)
                    chains.append(quant_steps(qb, src, D, [(P_s, srcs[e])],
                                              extra_ap=RSTD[:, t:t + 1]))
                toks = run_chains(chains)
                for e in range(2):
                    t = t0p + e
                    qb = base + e
                    st_["qb_free_v"][qb] = toks[e]
                    sc.op(lambda q, t=t, qb=qb: q.dma_start(
                        out=xq_scr[(t // 2) * 2048:(t // 2) * 2048 + 2048, :]
                        .rearrange("(c p) f -> c p f", p=256)
                        [0:8, (t % 2) * 128:(t % 2) * 128 + 128, :]
                        .rearrange("c p f -> p c f"),
                        in_=QO_[qb][:, 0:D].rearrange("p (c f) -> p c f", f=128)),
                        waits=[(P_v, toks[e])], inc=D_a)
                    st_["qb_store_da"][qb] = D_a.n
                tok_pair_store = D_a.n
                pr = t0p // 2
                sy.op(lambda q, pr=pr: q.dma_start(
                    out=XQF[:, pr * 2048:(pr + 1) * 2048],
                    in_=xq_scr[pr * 2048:(pr + 1) * 2048, :],
                    transpose=True),
                    waits=[(D_a, tok_pair_store), (P_t, st_["xqf_free_t"])],
                    inc=D_s)
                xqf_tok[t0p] = xqf_tok[t0p + 1] = D_s.n

            hq_done = [[0] * len(GATE_CH) for _ in range(2)]
            banksPS47 = [PS4[:, 0:512], PS5[:, 0:512], PS6[:, 0:512],
                         PS7[:, 0:512]]
            banksPSB = [PSB[:, 0:512], PSB[:, 512:1024], PSB[:, 1024:1536],
                        PSB[:, 1536:2048]]
            hqf_free = [0, 0]

            def emit_wdf(half, free_tok):
                wq_wait = st_["wq_tok_d"].get(D // 128 - 1, 0) if first else 0
                sy.op(lambda q, half=half: q.dma_start(
                    out=WDF[:, 0:NOFC * 512],
                    in_=wdq_scr[half * 11264:(half + 1) * 11264, :],
                    transpose=True),
                    waits=[(P_t, free_tok), (D_a, wq_wait)],
                    inc=D_s)
                return D_s.n

            def emit_down_group(sg, half, quad, wdf_tok, banks, bank_waits):
                tbase = sg * 8 + quad * 4
                tg = tbase // 4
                hq_tok = hq_done[sg][len(GATE_CH) - 1]
                pan_tok = [0] * 6
                def load_panel(p):
                    if p > 5:
                        return
                    w = (min(4 * p + 4, NOFC) - 4 * p) * 512
                    sy.op(lambda q, p=p, tg=tg, w=w: q.dma_start(
                        out=HQF[p % 2][:, 0:w],
                        in_=hq_scr[tg * 11264 + p * 2048:
                                   tg * 11264 + p * 2048 + w, :],
                        transpose=True),
                        waits=[(D_a, hq_tok), (P_t, hqf_free[p % 2])], inc=D_s)
                    pan_tok[p] = D_s.n
                load_panel(0)
                load_panel(1)
                for ofc in range(NOFC):
                    p = ofc // 4
                    col = (ofc % 4) * 512
                    for j in range(4):
                        w0 = [(D_s, pan_tok[p])] \
                            if (ofc % 4 == 0 and j == 0) else []
                        if ofc == 0 and j == 0:
                            w0 += [(D_s, wdf_tok)] + list(bank_waits)
                        te.op(lambda q, j=j, p=p, col=col, ofc=ofc, banks=banks:
                              q.matmul(
                                  banks[j],
                                  HQF[p % 2][:, col + j * 128:
                                              col + (j + 1) * 128],
                                  WDF[:, ofc * 512:(ofc + 1) * 512],
                                  start=(ofc == 0), stop=(ofc == NOFC - 1)),
                              waits=w0, inc=(P_t if (j == 3) else None))
                    if ofc % 4 == 3 or ofc == NOFC - 1:
                        hqf_free[p % 2] = P_t.n
                        load_panel(p + 2)
                mm_tok = P_t.n

                def evac():
                    for j in range(4):
                        t = tbase + j
                        xs = X[:, t * D + half * 512:t * D + half * 512 + 512]
                        ve.op(lambda q, xs=xs, banks=banks, j=j: q.tensor_tensor(
                            xs, banks[j], xs, ALU.add),
                            waits=[(P_t, mm_tok), (P_v, P_v.n)], inc=P_v)
                    return P_v.n
                return evac, mm_tok

            ds_ = {"ps47": 0, "wdf": 0}
            ev_g = [None] * 8
            mm_g = [0] * 8
            def emit_chunk_loads(sg, ci):
                coff, cw = GATE_CH[ci]
                wpar = ci % 2
                wq_wait = st_["wq_tok_u"].get((coff + cw) // 128 - 1, 0) \
                    if first else 0
                sy.op(lambda q, ci=ci, cw=cw, wpar=wpar: q.dma_start(
                    out=WGF[wpar][:, 0:8 * cw],
                    in_=wgq_scr[REGW[ci]:REGW[ci] + 8 * cw, :],
                    transpose=True),
                    waits=[(P_t, st_["wgf_free_t"][wpar]), (D_a, wq_wait)],
                    inc=D_s)
                sy.op(lambda q, ci=ci, cw=cw, wpar=wpar: q.dma_start(
                    out=WUF[wpar][:, 0:8 * cw],
                    in_=wuq_scr[REGW[ci]:REGW[ci] + 8 * cw, :],
                    transpose=True),
                    inc=D_s)
                return D_s.n

            tok_w_next = [0]
            for sg in range(2):
                for ci, (coff, cw) in enumerate(GATE_CH):
                    wpar = ci % 2
                    if sg == 0 and ci == 0:
                        tok_w = emit_chunk_loads(0, 0)
                    else:
                        tok_w = tok_w_next[0]
                    if ci < len(GATE_CH) - 1:
                        if first:
                            emit_wpairs(max(0, min(4 * ci + 8, NWT) - wcur[0]))
                        tok_w_next[0] = emit_chunk_loads(sg, ci + 1)
                    elif sg == 0:
                        tok_w_next[0] = emit_chunk_loads(1, 0)
                    tok_w_a = 0
                    for t0p in range(sg * 8, sg * 8 + 8, 2):
                        base = (t0p // 2 % 2) * 2
                        hsrc = []
                        for e in range(2):
                            t = t0p + e
                            ph = e
                            g_ps = PSB[:, ph * 512:ph * 512 + cw]
                            u_ps = PSB[:, 1024 + ph * 512:1024 + ph * 512 + cw]
                            for kc in range(NKC):
                                w0 = []
                                if kc == 0:
                                    w0 = [(D_s, tok_w), (D_a, tok_w_a),
                                          (D_s, xqf_tok[t]),
                                          (P_s, st_["gps_free_s"][ph]),
                                          (P_v, st_["ups_free_v"][ph])]
                                te.op(lambda q, kc=kc, t=t, cw=cw, g_ps=g_ps,
                                      wpar=wpar: q.matmul(
                                          g_ps,
                                          XQF[:, (t // 2) * 2048 + kc * 256 +
                                              (t % 2) * 128:
                                              (t // 2) * 2048 + kc * 256 +
                                              (t % 2) * 128 + 128],
                                          WGF[wpar][:, kc * cw:kc * cw + cw],
                                          start=(kc == 0), stop=(kc == NKC - 1)),
                                      waits=w0, inc=(P_t if kc == NKC - 1 else None))
                            for kc in range(NKC):
                                te.op(lambda q, kc=kc, t=t, cw=cw, u_ps=u_ps,
                                      wpar=wpar: q.matmul(
                                          u_ps,
                                          XQF[:, (t // 2) * 2048 + kc * 256 +
                                              (t % 2) * 128:
                                              (t // 2) * 2048 + kc * 256 +
                                              (t % 2) * 128 + 128],
                                          WUF[wpar][:, kc * cw:kc * cw + cw],
                                          start=(kc == 0), stop=(kc == NKC - 1)),
                                      inc=(P_t if kc == NKC - 1 else None))
                            tok_mm = P_t.n
                            if t == sg * 8 + 7:
                                st_["wgf_free_t"][wpar] = tok_mm
                            sc.op(lambda q, e=e, cw=cw, g_ps=g_ps: q.activation(
                                SGT[e][:, 0:cw], g_ps, AF.Silu),
                                waits=[(P_t, tok_mm), (P_v, st_["sgt_free_v"][e]),
                                       (P_s, P_s.n)], inc=P_s)
                            st_["gps_free_s"][ph] = P_s.n
                            tok_sg = P_s.n
                            ve.op(lambda q, e=e, cw=cw, u_ps=u_ps: q.tensor_tensor(
                                HT[e][:, 0:cw], SGT[e][:, 0:cw], u_ps, ALU.mult),
                                waits=[(P_s, tok_sg), (P_v, st_["ht_free_v"][e]),
                                       (P_v, P_v.n)], inc=P_v)
                            st_["ups_free_v"][ph] = P_v.n
                            st_["sgt_free_v"][e] = P_v.n
                            hsrc.append(P_v.n)
                        chains = [quant_steps(base + e, HT[e][:, 0:cw], cw,
                                              [(P_v, hsrc[e])]) for e in range(2)]
                        toks = run_chains(chains)
                        nsl = cw // 128
                        for e in range(2):
                            t = t0p + e
                            qb = base + e
                            st_["ht_free_v"][e] = toks[e]
                            st_["qb_free_v"][qb] = toks[e]
                            sc.op(lambda q, qb=qb, coff=coff, nsl=nsl, t=t, cw=cw:
                                  q.dma_start(
                                      out=hq_scr[(t // 4) * 11264:
                                                 (t // 4 + 1) * 11264, :]
                                      .rearrange("(c p) f -> c p f", p=512)
                                      [coff // 128:coff // 128 + nsl,
                                       (t % 4) * 128:(t % 4) * 128 + 128, :]
                                      .rearrange("c p f -> p c f"),
                                      in_=QO_[qb][:, 0:cw]
                                      .rearrange("p (c f) -> p c f", f=128)),
                                  waits=[(P_v, toks[e]), (D_s, st_["hq_read_ds"])],
                                  inc=D_a)
                            st_["qb_store_da"][qb] = D_a.n
                    if first:
                        emit_wpairs(4)
                    hq_done[sg][ci] = D_a.n
                    if sg == 1 and ci == 0:
                        ds_["wdf"] = emit_wdf(0, st_["wdf_free_t"])
                        ev_g[0], mm_g[0] = emit_down_group(
                            0, 0, 0, ds_["wdf"], banksPS47,
                            [(P_v, ds_["ps47"])])
                    elif sg == 1 and ci == 1:
                        ds_["ps47"] = ev_g[0]()
                        ev_g[1], mm_g[1] = emit_down_group(
                            0, 0, 1, ds_["wdf"], banksPS47,
                            [(P_v, ds_["ps47"])])
                    elif sg == 1 and ci == 2:
                        ds_["ps47"] = ev_g[1]()
                        ds_["wdf"] = emit_wdf(1, mm_g[1])
                        ev_g[2], mm_g[2] = emit_down_group(
                            0, 1, 0, ds_["wdf"], banksPS47,
                            [(P_v, ds_["ps47"])])
                    elif sg == 1 and ci == 3:
                        ds_["ps47"] = ev_g[2]()
                        ev_g[3], mm_g[3] = emit_down_group(
                            0, 1, 1, ds_["wdf"], banksPS47,
                            [(P_v, ds_["ps47"])])
                    elif sg == 1 and ci == 4:
                        ds_["ps47"] = ev_g[3]()
                        st_["x_half0_v"] = ds_["ps47"]
                if sg == 0 and first:
                    emit_wpairs(len(wpairs))
            st_["xqf_free_t"] = P_t.n

            psb_waits = [(P_s, st_["gps_free_s"][0]), (P_s, st_["gps_free_s"][1]),
                         (P_v, st_["ups_free_v"][0]), (P_v, st_["ups_free_v"][1])]
            tok_wdf = emit_wdf(0, mm_g[3])
            ev_g[4], mm_g[4] = emit_down_group(1, 0, 0, tok_wdf, banksPS47,
                                               [(P_v, ds_["ps47"])])
            ev_g[5], mm_g[5] = emit_down_group(1, 0, 1, tok_wdf, banksPSB,
                                               psb_waits)
            ps47f = ev_g[4]()
            tok_wdf = emit_wdf(1, mm_g[5])
            ev_g[6], mm_g[6] = emit_down_group(1, 1, 0, tok_wdf, banksPS47,
                                               [(P_v, ps47f)])
            psbf = ev_g[5]()
            ev_g[7], mm_g[7] = emit_down_group(1, 1, 1, tok_wdf, banksPSB,
                                               [(P_v, psbf)])
            ev_g[6]()
            ev_g[7]()
            st_["wdf_free_t"] = mm_g[7]
            st_["hq_read_ds"] = D_s.n
            st_["x_tok_v"] = P_v.n

        def emit_attention(ps):
            st_["x_half0_v"] = 0
            astack = ps

            def sbt(name, shape, dtype):
                return astack.enter_context(nc.sbuf_tensor(name, shape, dtype))
            WKB = sbt("WKB", [128, NKC * 1024], BF16)
            XN2F = sbt("XN2F", [128, NKC * 2048], BF16)
            KF = sbt("KF", [128, NKC * 2048], BF16)
            VT = sbt("VT", [128, NT * 1024], BF16)
            BOUTB = sbt("BOUTB", [1, D], BF16)
            R2 = sbt("R2", [128, 4], F32)
            RMX = sbt("RMX", [128, 16], F32)
            LP = sbt("LP", [128, 48], F32)
            BX = sbt("BX", [128, 3], F32)
            RL = sbt("RL", [128, 3], F32)
            NMX = sbt("NMX", [128, 1], F32)
            LL = sbt("LL", [128, 1], F32)
            inner = ExitStack()
            GAM = inner.enter_context(nc.sbuf_tensor("GAM", [128, D], F32))
            BET = inner.enter_context(nc.sbuf_tensor("BET", [128, D], F32))
            XN1 = [inner.enter_context(nc.sbuf_tensor(f"XN1{i}", [128, D], F32))
                   for i in range(2)]
            XN1B = [inner.enter_context(nc.sbuf_tensor(f"XN1B{i}", [128, D], F32))
                    for i in range(2)]
            XN2B = [inner.enter_context(nc.sbuf_tensor("XN2B0", [128, D], BF16))]
            XN2B.append(XN2B[0])

            tok_rstd1, tok_nmr1 = emit_ln_prepass(want_nmr=True)
            sc.op(lambda q: q.dma_start(out=GAM[:], in_=gam_in), inc=D_a)
            sc.op(lambda q: q.dma_start(out=BET[:], in_=bet_in), inc=D_a)
            tok_gb = D_a.n
            gp.op(lambda q: q.dma_start(out=BOUTB[:], in_=bout_in),
                  waits=[(P_g, P_g.n)], inc=D_g)
            tok_bout = D_g.n
            wqkv3 = wqkvT_in.rearrange("(c p) e -> p c e", p=128)
            gp.op(lambda q: q.dma_start(
                out=WKB[:].rearrange("p (c f) -> p c f", c=NKC),
                in_=wqkv3[:, :, D:2 * D]), inc=D_g)
            tok_wk = D_g.n

            QRING = [PSB[:, 0:512], PSB[:, 512:1024], PSB[:, 1024:1536],
                     PSB[:, 1536:2048], PS4[:, 0:512], PS5[:, 0:512],
                     PS6[:, 0:512], PS7[:, 0:512]]
            ps_free = [0] * 8
            xn1_free_s = [0, 0]
            xn2b_free_a = [0, 0]
            xn2_store_h = [0, 0]
            xn2_store_q = [0, 0, 0, 0]
            xn2f_qtok = [0, 0, 0, 0]
            pend_kevac = []

            def flush_kevac():
                for (kf, tg4, ps, pp, tok_kmm) in pend_kevac:
                    sc.op(lambda q, kf=kf, tg4=tg4, ps=ps: q.activation(
                        KF[:, kf * 2048 + tg4 * 512:
                           kf * 2048 + (tg4 + 1) * 512],
                        ps, AF.Copy),
                        waits=[(P_t, tok_kmm)], inc=P_s)
                    ps_free[pp] = P_s.n
                del pend_kevac[:]
            for t in range(NT):
                par = t % 2
                sc.op(lambda q, t=t, par=par: q.activation(
                    XN1[par][:], X[:, t * D:(t + 1) * D], AF.Identity,
                    scale=RSTD[:, t:t + 1], bias=NMR[:, t:t + 1]),
                    waits=[(P_s, tok_rstd1), (P_v, tok_nmr1),
                           (P_v, xn1_free_s[par]), (P_s, P_s.n)], inc=P_s)
                tok_x1 = P_s.n
                ve.op(lambda q, par=par: q.bn_stats(STT[:, 0:6], XN1[par][:, 0:512]),
                      waits=[(P_s, tok_x1), (P_v, P_v.n)], inc=P_v)
                ve.op(lambda q, par=par: q.bn_stats(
                    STT[:, 6:12], XN1[par][:, 512:1024]),
                    waits=[(P_v, P_v.n)], inc=P_v)
                ve.op(lambda q, par=par: q.bn_aggr(
                    R2[:, 2 * par:2 * par + 2], STT[:]),
                    waits=[(P_v, P_v.n)], inc=P_v)
                tok_st2 = P_v.n
                sc.op(lambda q, par=par: q.activation(
                    LNV[:, par:par + 1], R2[:, 2 * par + 1:2 * par + 2], AF.Ln,
                    bias=EPS1[:]),
                    waits=[(P_v, tok_st2), (P_s, P_s.n)], inc=P_s)
                sc.op(lambda q, par=par: q.activation(
                    BX[:, par:par + 1], LNV[:, par:par + 1], AF.Exp, scale=-0.5),
                    waits=[(P_s, P_s.n)], inc=P_s)
                tok_r2 = P_s.n
                ve.op(lambda q, par=par: q.scalar_tensor_tensor(
                    NMX[:], R2[:, 2 * par:2 * par + 1], -1.0,
                    BX[:, par:par + 1], ALU.mult, ALU.mult),
                    waits=[(P_s, tok_r2), (P_v, P_v.n)], inc=P_v)
                tok_nm2 = P_v.n
                sc.op(lambda q, par=par: q.activation(
                    XN1B[par][:], XN1[par][:], AF.Identity,
                    scale=BX[:, par:par + 1], bias=NMX[:]),
                    waits=[(P_v, tok_nm2), (P_s, P_s.n)], inc=P_s)
                tok_x2a = P_s.n
                xn1_free_s[par] = tok_x2a
                ve.op(lambda q, par=par: q.tensor_tensor(
                    XN1[par][:], XN1B[par][:], GAM[:], ALU.mult),
                    waits=[(P_s, tok_x2a), (D_a, tok_gb), (P_v, P_v.n)], inc=P_v)
                ve.op(lambda q, par=par: q.tensor_tensor(
                    XN2B[par][:], XN1[par][:], BET[:], ALU.add),
                    waits=[(P_v, P_v.n), (D_g, xn2b_free_a[par])], inc=P_v)
                tok_x2 = P_v.n
                gp.op(lambda q, t=t, par=par: q.dma_start(
                    out=xn2_scr[:, t * 128:(t + 1) * 128, :]
                    .rearrange("c p f -> p c f"),
                    in_=XN2B[par][:].rearrange("p (c f) -> p c f", f=128)),
                    waits=[(P_v, tok_x2)], inc=D_g)
                xn2b_free_a[par] = D_g.n
                if t % 4 == 3:
                    tg4 = t // 4
                    xn2_store_q[tg4] = D_g.n
                    flush_kevac()
                    for kc in range(NKC):
                        sy.op(lambda q, kc=kc, tg4=tg4: q.dma_start(
                            out=XN2F[:, kc * 2048 + tg4 * 512:
                                     kc * 2048 + tg4 * 512 + 512],
                            in_=xn2_scr[kc, tg4 * 512:(tg4 + 1) * 512, :],
                            transpose=True),
                            waits=[(D_g, xn2_store_q[tg4])], inc=D_s)
                    xn2f_qtok[tg4] = D_s.n
                    for kf in range(NKC):
                        pp = kf
                        ps = QRING[pp]
                        for kc in range(NKC):
                            w0 = []
                            if kc == 0:
                                w0 = [(D_g, tok_wk),
                                      (D_s, xn2f_qtok[tg4]),
                                      (P_s, ps_free[pp])]
                            te.op(lambda q, kc=kc, kf=kf, tg4=tg4, ps=ps:
                                  q.matmul(
                                      ps,
                                      WKB[:, kc * 1024 + kf * 128:
                                          kc * 1024 + (kf + 1) * 128],
                                      XN2F[:, kc * 2048 + tg4 * 512:
                                           kc * 2048 + (tg4 + 1) * 512],
                                      start=(kc == 0), stop=(kc == NKC - 1)),
                                  waits=w0,
                                  inc=(P_t if kc == NKC - 1 else None))
                        pend_kevac.append((kf, tg4, ps, pp, P_t.n))
            xn2_store_h[1] = D_g.n
            inner.close()
            qstack = ExitStack()
            WTMP = qstack.enter_context(
                nc.sbuf_tensor("WTMP", [128, NKC * 1024], BF16))
            gp.op(lambda q: q.dma_start(
                out=WTMP[:].rearrange("p (c f) -> p c f", c=NKC),
                in_=wqkv3[:, :, 2 * D:3 * D]),
                waits=[(D_g, xn2_store_h[1])], inc=D_g)
            tok_wv = D_g.n

            flush_kevac()
            tok_k_done_t = P_t.n
            gp.op(lambda q: q.dma_start(
                out=WKB[:].rearrange("p (c f) -> p c f", c=NKC),
                in_=wqkv3[:, :, 0:D]),
                waits=[(P_t, tok_k_done_t)], inc=D_g)
            tok_wq = D_g.n
            for t in range(NT):
                for vc in range(2):
                    pp = (t * 2 + vc) % 8
                    ps = QRING[pp]
                    for kc in range(NKC):
                        w0 = []
                        if kc == 0:
                            w0 = [(D_g, tok_wv), (P_s, ps_free[pp])]
                        te.op(lambda q, kc=kc, t=t, vc=vc, ps=ps: q.matmul(
                            ps,
                            XN2F[:, kc * 2048 + t * 128:kc * 2048 + (t + 1) * 128],
                            WTMP[:, kc * 1024 + vc * 512:kc * 1024 + (vc + 1) * 512],
                            start=(kc == 0), stop=(kc == NKC - 1)),
                            waits=w0, inc=(P_t if kc == NKC - 1 else None))
                    tok_mm = P_t.n
                    sc.op(lambda q, t=t, vc=vc, ps=ps: q.activation(
                        VT[:, t * 1024 + vc * 512:t * 1024 + (vc + 1) * 512],
                        ps, AF.Copy),
                        waits=[(P_t, tok_mm)], inc=P_s)
                    ps_free[pp] = P_s.n
            tok_v_done_t = P_t.n
            x3q = XN2F[:].rearrange("p (c n) -> p c n", c=NKC)
            for t in range(NT):
                for qf in range(NKC):
                    pp = (t * 2 + qf // 4) % 8
                    ps = QRING[pp]
                    qq = qf % 4
                    for kc in range(NKC):
                        w0 = []
                        if kc == 0 and qq == 0:
                            w0 = [(D_g, tok_wq), (P_s, ps_free[pp])]
                        te.op(lambda q, kc=kc, t=t, qf=qf, ps=ps, qq=qq: q.matmul(
                            ps[:, qq * 128:(qq + 1) * 128],
                            WKB[:, kc * 1024 + qf * 128:kc * 1024 + (qf + 1) * 128],
                            XN2F[:, kc * 2048 + t * 128:kc * 2048 + (t + 1) * 128],
                            start=(kc == 0), stop=(kc == NKC - 1)),
                            waits=w0, inc=(P_t if kc == NKC - 1 else None))
                tok_mm = P_t.n
                for half in range(2):
                    pp = (t * 2 + half) % 8
                    ps = QRING[pp]
                    sc.op(lambda q, t=t, half=half, ps=ps: q.activation(
                        x3q[:, 4 * half:4 * half + 4, t * 128:(t + 1) * 128],
                        ps.rearrange("p (c n) -> p c n", c=4),
                        AF.Copy, scale=0.125),
                        waits=[(P_t, tok_mm), (P_s, P_s.n)], inc=P_s)
                    ps_free[pp] = P_s.n
            tok_q_done_s = P_s.n
            tok_q_done_t = P_t.n
            qstack.close()
            gp.op(lambda q: q.dma_start(
                out=WKB[:].rearrange("p (c f) -> p c f", c=NKC),
                in_=woutT_in.rearrange("(c p) e -> p c e", p=128)),
                waits=[(P_t, tok_q_done_t)], inc=D_g)
            tok_wo = D_g.n
            ASB = [sbt(f"ASB{i}", [128, 2048], BF16) for i in range(3)]
            ATS = [sbt(f"ATS{i}", [128, 1024], BF16) for i in range(2)]
            OTM = [sbt(f"OTM{i}", [128, 1024], BF16) for i in range(2)]
            OFM = [sbt(f"OFM{i}", [128, 1024], BF16) for i in range(2)]

            slot_free_s = [tok_q_done_s] * 4
            pso_free_s = [(P_s, tok_q_done_s), (P_s, tok_q_done_s)]
            asb_free_t = [0, 0, 0]
            bx_free_s = [0, 0, 0]
            rl_free_s = [(P_s, 0), (P_s, 0), (P_s, 0)]
            lp_free_v = [0, 0, 0]
            otm_store_a = [0, 0]
            ofm_free_t = [0, 0]
            gci = 0
            TB = [PS4B, PS7B]
            TBF = [PS4, PS7]
            tb_evac = [(P_s, tok_q_done_s), (P_s, tok_q_done_s)]
            tb_ring = [0]
            ats_av_t = [0, 0]
            k3 = KF[:].rearrange("p (c n) -> p c n", c=NKC)
            v3 = VT[:].rearrange("p (c n) -> p c n", c=NT)

            def stage_S(qt, h, vg):
                nonlocal gci
                ntk = qt + 1
                W = ntk * 128
                hb = (h % 2) * 64
                hc = h // 2
                apar = vg % 3
                chunks = []
                koff = 0
                while koff < W:
                    w = min(512, W - koff)
                    chunks.append((koff, w))
                    koff += w
                nch = len(chunks)
                slot0 = gci
                merged = nch <= 2 and (slot0 % 4) + nch <= 4
                for j, (ko, w) in enumerate(chunks):
                    slot = gci % 4
                    gci += 1
                    sp0 = slot * 512
                    last = (j == nch - 1)
                    te.op(lambda q, ko=ko, w=w, sp0=sp0, hb=hb, hc=hc, qt=qt,
                          last=last: q.matmul(
                              PSB[:, sp0:sp0 + w],
                              x3q[hb:hb + 64, hc, qt * 128:(qt + 1) * 128],
                              k3[hb:hb + 64, hc, ko:ko + w],
                              start=True, stop=not last),
                          waits=[(P_s, slot_free_s[slot])],
                          inc=(None if last else P_t))
                    if last:
                        te.op(lambda q, sp0=sp0, w=w: q.matmul(
                            PSB[:, sp0 + w - 128:sp0 + w],
                            IDN[:], MASKB[:],
                            start=False, stop=True),
                            waits=[(P_g, TOK_MASK)], inc=P_t)
                    tok_smm = P_t.n
                    if not merged:
                        ve.op(lambda q, sp0=sp0, w=w, j=j: q.tensor_reduce(
                            RMX[:, j:j + 1], PSB[:, sp0:sp0 + w], AX.X, ALU.max),
                            waits=[(P_t, tok_smm)],
                            inc=(P_v if j == nch - 1 else None))
                if merged:
                    s0 = (slot0 % 4) * 512
                    ve.op(lambda q, s0=s0, W=W, apar=apar: q.tensor_reduce(
                        BX[:, apar:apar + 1], PSB[:, s0:s0 + W], AX.X, ALU.max,
                        negate=True),
                        waits=[(P_t, tok_smm), (P_v, P_v.n),
                               (P_s, bx_free_s[apar])], inc=P_v)
                else:
                    ve.op(lambda q, nch=nch, apar=apar: q.tensor_reduce(
                        BX[:, apar:apar + 1], RMX[:, 0:nch], AX.X, ALU.max,
                        negate=True),
                        waits=[(P_v, P_v.n), (P_s, bx_free_s[apar])], inc=P_v)
                tok_bx = P_v.n
                if merged:
                    s0 = (slot0 % 4) * 512
                    sc.op(lambda q, s0=s0, W=W, apar=apar: q.activation(
                        ASB[apar][:, 0:W], PSB[:, s0:s0 + W],
                        AF.Exp,
                        bias=BX[:, apar:apar + 1],
                        accum_out=LP[:, apar * 16:apar * 16 + 1]),
                        waits=[(P_v, tok_bx), (P_t, asb_free_t[apar]),
                               (P_v, lp_free_v[apar])], inc=P_s)
                    for j in range(nch):
                        slot_free_s[(slot0 + j) % 4] = P_s.n
                else:
                    for j, (ko, w) in enumerate(chunks):
                        slot = (slot0 + j) % 4
                        sp0 = slot * 512
                        sc.op(lambda q, ko=ko, w=w, sp0=sp0, j=j, apar=apar:
                              q.activation(
                                  ASB[apar][:, ko:ko + w], PSB[:, sp0:sp0 + w],
                                  AF.Exp,
                                  bias=BX[:, apar:apar + 1],
                                  accum_out=LP[:, apar * 16 + j:
                                               apar * 16 + j + 1]),
                              waits=([(P_v, tok_bx), (P_t, asb_free_t[apar]),
                                      (P_v, lp_free_v[apar])] if j == 0
                                     else [(P_v, tok_bx)]), inc=P_s)
                        slot_free_s[slot] = P_s.n
                tok_exp = P_s.n
                bx_free_s[apar] = tok_exp
                return {"qt": qt, "h": h, "ntk": ntk, "nch": nch, "apar": apar,
                        "opar": vg % 2, "tok_exp": tok_exp,
                        "merged": merged}

            def stage_T(c):
                h, ntk, apar, tok_exp = c["h"], c["ntk"], c["apar"], c["tok_exp"]
                nch = c["nch"]
                if c["merged"]:
                    ve.op(lambda q, apar=apar: q.reciprocal(
                        RL[:, apar:apar + 1], LP[:, apar * 16:apar * 16 + 1]),
                        waits=[(P_s, tok_exp), (P_v, P_v.n),
                               rl_free_s[apar]], inc=P_v)
                    lp_free_v[apar] = P_v.n
                else:
                    ve.op(lambda q, nch=nch, apar=apar: q.tensor_reduce(
                        LL[:], LP[:, apar * 16:apar * 16 + nch], AX.X, ALU.add),
                        waits=[(P_s, tok_exp), (P_v, P_v.n)], inc=P_v)
                    lp_free_v[apar] = P_v.n
                    ve.op(lambda q, apar=apar: q.reciprocal(
                        RL[:, apar:apar + 1], LL[:]),
                        waits=[(P_v, P_v.n), rl_free_s[apar]], inc=P_v)
                c["tok_rl"] = P_v.n
                nbat = (ntk + 7) // 8
                bats = []
                for b in range(nbat):
                    b0 = b * 8
                    bw = min(8, ntk - b0)
                    sl = tb_ring[0]
                    tb_ring[0] ^= 1
                    for j2 in range(bw):
                        w0 = []
                        if j2 == 0:
                            w0 = [(P_s, tok_exp), tb_evac[sl],
                                  (P_g, TOK_IDN)]
                        te.op(lambda q, j2=j2, b0=b0, sl=sl, apar=apar:
                              q.transpose(
                                  TB[sl][:, j2 * 128:(j2 + 1) * 128],
                                  ASB[apar][:, (b0 + j2) * 128:
                                            (b0 + j2 + 1) * 128],
                                  IDN[:]),
                              waits=w0, inc=(P_t if j2 == bw - 1 else None))
                    bats.append((b0, bw, sl, P_t.n))
                c["bats"] = bats
                asb_free_t[apar] = P_t.n
                return c

            def stage_C(c):
                h = c["h"]
                nb = []
                for bi, (b0, bw, sl, tok_tp) in enumerate(c["bats"]):
                    if (h + bi) % 2 == 0:
                        sc.op(lambda q, bw=bw, sl=sl: q.activation(
                            ATS[sl][:, 0:bw * 128],
                            TB[sl][:, 0:bw * 128], AF.Copy),
                            waits=[(P_t, tok_tp), (P_t, ats_av_t[sl]),
                                   (P_s, P_s.n)], inc=P_s)
                        ev = (P_s, P_s.n)
                    else:
                        ve.op(lambda q, bw=bw, sl=sl: q.tensor_copy(
                            ATS[sl][:, 0:bw * 128],
                            TB[sl][:, 0:bw * 128]),
                            waits=[(P_t, tok_tp), (P_t, ats_av_t[sl]),
                                   (P_v, P_v.n)], inc=P_v)
                        ev = (P_v, P_v.n)
                    tb_evac[sl] = ev
                    nb.append((b0, bw, sl, ev))
                c["bats"] = nb
                return c

            def stage_AV(c):
                h, ntk, opar = c["h"], c["ntk"], c["opar"]
                pso = PS5 if opar == 0 else PS6
                first = True
                for (b0, bw, sl, ev) in c["bats"]:
                    for j2 in range(bw):
                        w0 = [ev] if j2 == 0 else []
                        if first and j2 == 0:
                            w0.append(pso_free_s[opar])
                        te.op(lambda q, j2=j2, b0=b0, sl=sl, h=h, pso=pso:
                              q.matmul(
                                  pso[:, 0:64],
                                  ATS[sl][:, j2 * 128:(j2 + 1) * 128],
                                  v3[:, b0 + j2, h * 64:(h + 1) * 64],
                                  start=(b0 + j2 == 0),
                                  stop=(b0 + j2 == ntk - 1)),
                              waits=w0,
                              inc=(P_t if j2 == bw - 1 else None))
                    ats_av_t[sl] = P_t.n
                    first = False
                c["tok_av"] = P_t.n
                return c

            def stage_OEV(c):
                h, apar, opar, qt = c["h"], c["apar"], c["opar"], c["qt"]
                pso = PS5 if opar == 0 else PS6
                ve.op(lambda q, h=h, pso=pso, apar=apar, qt=qt: q.tensor_scalar(
                    OTM[qt % 2][:, h * 64:(h + 1) * 64], pso[:, 0:64],
                    RL[:, apar:apar + 1], None, ALU.mult),
                    waits=[(P_t, c["tok_av"]), (P_v, c["tok_rl"]),
                           (D_a, otm_store_a[qt % 2]), (P_v, P_v.n)], inc=P_v)
                pso_free_s[opar] = (P_v, P_v.n)
                rl_free_s[apar] = (P_v, P_v.n)

            def emit_outproj(octx):
                qt, tok_ofm, opar2 = octx
                sl = tb_ring[0]
                tb_ring[0] ^= 1
                for half in range(2):
                    for kc in range(NKC):
                        w0 = []
                        if kc == 0:
                            w0 = [(D_s, tok_ofm), (D_g, tok_wo)]
                            if half == 0:
                                w0.append(tb_evac[sl])
                            else:
                                w0.append((P_v, halftok[0]))
                        te.op(lambda q, kc=kc, half=half, opar2=opar2, sl=sl:
                              q.matmul(
                                  TBF[sl][:, 0:512],
                                  OFM[opar2][:, kc * 128:(kc + 1) * 128],
                                  WKB[:, kc * 1024 + half * 512:
                                      kc * 1024 + (half + 1) * 512],
                                  start=(kc == 0), stop=False),
                              waits=w0)
                    te.op(lambda q, half=half, sl=sl: q.matmul(
                        TBF[sl][:, 0:512], ONES1[:],
                        BOUTB[0:1, half * 512:(half + 1) * 512],
                        start=False, stop=True),
                        waits=[(P_g, TOK_ONES), (D_g, tok_bout)], inc=P_t)
                    tok_op = P_t.n
                    if half == 1:
                        ofm_free_t[opar2] = tok_op
                    xs = X[:, qt * D + half * 512:qt * D + half * 512 + 512]
                    ve.op(lambda q, xs=xs, sl=sl: q.tensor_tensor(
                        xs, TBF[sl][:, 0:512], xs, ALU.add),
                        waits=[(P_t, tok_op), (P_v, P_v.n)], inc=P_v)
                    halftok[0] = P_v.n
                tb_evac[sl] = (P_v, P_v.n)

            halftok = [0]
            pend_ops = []
            vg = 0
            for qp in range(0, NT, 2):
                vlist = []
                for h in range(HEADS):
                    vlist.append((qp, h))
                    vlist.append((qp + 1, h))
                p1, p2, p3, p4 = [], [], [], []
                for vi, (qt, h) in enumerate(vlist):
                    if p4:
                        stage_OEV(p4.pop(0))
                    if p3:
                        p4.append(stage_AV(p3.pop(0)))
                    if p2:
                        p3.append(stage_C(p2.pop(0)))
                    if vi in (4, 10) and pend_ops:
                        emit_outproj(pend_ops.pop(0))
                    sctx = stage_S(qt, h, vg)
                    vg += 1
                    if p1:
                        p2.append(stage_T(p1.pop(0)))
                    p1.append(sctx)
                while p1 or p2 or p3 or p4:
                    if p4:
                        stage_OEV(p4.pop(0))
                    if p3:
                        p4.append(stage_AV(p3.pop(0)))
                    if p2:
                        p3.append(stage_C(p2.pop(0)))
                    if p1:
                        p2.append(stage_T(p1.pop(0)))
                tok_o = P_v.n
                for qt in (qp, qp + 1):
                    sc.op(lambda q, qt=qt: q.dma_start(
                        out=o_scr[qt * 1024:(qt + 1) * 1024, :]
                        .rearrange("(c p) f -> c p f", p=128)
                        .rearrange("c p f -> p c f"),
                        in_=OTM[qt % 2][:].rearrange("p (c f) -> p c f", f=128)),
                        waits=[(P_v, tok_o), (P_s, P_s.n)], inc=D_a)
                    otm_store_a[qt % 2] = D_a.n
                    opar2 = qt % 2
                    sy.op(lambda q, qt=qt, opar2=opar2: q.dma_start(
                        out=OFM[opar2][:, 0:1024],
                        in_=o_scr[qt * 1024:(qt + 1) * 1024, :],
                        transpose=True),
                        waits=[(D_a, otm_store_a[qt % 2]),
                               (P_t, ofm_free_t[opar2])],
                        inc=D_s)
                    pend_ops.append((qt, D_s.n, opar2))
            for octx in pend_ops:
                emit_outproj(octx)
            st_["x_tok_v"] = P_v.n

        def ffn_stack(ps, tag=""):
            def sbt(name, shape, dtype):
                return ps.enter_context(nc.sbuf_tensor(name + tag, shape, dtype))
            return {
                "XQF": sbt("XQF", [128, NKC * 2048], BF16),
                "HQF": [sbt(f"HQF{i}", [128, 2048], BF16) for i in range(2)],
                "WGF": [sbt(f"WGF{i}", [128, NKC * 512], BF16) for i in range(2)],
                "WUF": [sbt(f"WUF{i}", [128, NKC * 512], BF16) for i in range(2)],
                "WDF": sbt("WDF", [128, NOFC * 512], BF16),
                "WRAW": sbt("WRAW", [128, 2048], F32),
                "SGT": [sbt(f"SGT{i}", [128, 512], F32) for i in range(2)],
                "QA": sbt("QA", [128, 2048], F32),
                "QB": sbt("QB", [128, 2048], F32),
                "QM": sbt("QM", [128, 2048], F32),
                "CAND": sbt("CAND", [128, 16], F32),
                "QS": [sbt(f"QS{i}", [128, 64], F32) for i in range(4)],
                "QI": [sbt(f"QI{i}", [128, 64], F32) for i in range(4)],
                "QD": [sbt(f"QD{i}", [128, 64], F32) for i in range(4)],
                "QO": [sbt(f"QO{i}", [128, 1024], BF16) for i in range(4)],
            }

        with ExitStack() as ps1:
            fs = ffn_stack(ps1)
            emit_ffn_sublayer(fs, first=True)
        if debug_stop != "ffn1":
            barrier()
            with ExitStack() as ps2:
                emit_attention(ps2)
            if debug_stop != "attn":
                barrier()
                with ExitStack() as ps3:
                    fs = ffn_stack(ps3, tag="b")
                    emit_ffn_sublayer(fs, first=False)

        if st_.get("x_half0_v"):
            sy.op(lambda q: q.dma_start(
                out=out[0:8 * 128, :].rearrange("(t p) f -> p t f", p=128),
                in_=X[:, 0:8 * D].rearrange("p (t f) -> p t f", t=8)),
                waits=[(P_v, st_["x_half0_v"])], inc=D_s)
            sy.op(lambda q: q.dma_start(
                out=out[8 * 128:S, :].rearrange("(t p) f -> p t f", p=128),
                in_=X[:, 8 * D:NT * D].rearrange("p (t f) -> p t f", t=8)),
                waits=all_counts(), inc=D_s)
        else:
            sy.op(lambda q: q.dma_start(
                out=out.rearrange("(t p) f -> p t f", p=128),
                in_=X[:].rearrange("p (t f) -> p t f", t=NT)),
                waits=all_counts(), inc=D_s)
        fin = D_s.n
        sy.ops.append(lambda q: q.wait_ge(D_s.h, fin))

        with nc.Block() as block:
            @block.sync
            def _(q):
                for f in sy.ops:
                    f(q)

            @block.vector
            def _(q):
                for f in ve.ops:
                    f(q)

            @block.scalar
            def _(q):
                for f in sc.ops:
                    f(q)

            @block.tensor
            def _(q):
                for f in te.ops:
                    f(q)

            @block.gpsimd
            def _(q):
                for f in gp.ops:
                    f(q)

    return nc


def kernel(L0_attn_x, L0_attn_ln_gamma, L0_attn_ln_beta, L0_attn_w_qkv,
           L0_attn_w_out, L0_attn_b_out, L0_ffn_w_gate, L0_ffn_w_up,
           L0_ffn_w_down, _debug_stop=None):
    x = np.ascontiguousarray(np.asarray(L0_attn_x, dtype=np.float32))
    wqkvT = np.ascontiguousarray(np.asarray(L0_attn_w_qkv, dtype=np.float32).T)
    woutT = np.ascontiguousarray(np.asarray(L0_attn_w_out, dtype=np.float32).T)
    bout = np.ascontiguousarray(np.asarray(L0_attn_b_out, dtype=np.float32)[None, :])
    gam = np.ascontiguousarray(
        np.broadcast_to(np.asarray(L0_attn_ln_gamma, dtype=np.float32), (128, D)))
    bet = np.ascontiguousarray(
        np.broadcast_to(np.asarray(L0_attn_ln_beta, dtype=np.float32), (128, D)))
    wg = np.ascontiguousarray(np.asarray(L0_ffn_w_gate, dtype=np.float32))
    wu = np.ascontiguousarray(np.asarray(L0_ffn_w_up, dtype=np.float32))
    wd = np.ascontiguousarray(np.asarray(L0_ffn_w_down, dtype=np.float32))

    nc = build_program(_debug_stop)
    in_maps = []
    for c in range(8):
        in_maps.append({
            "x": np.ascontiguousarray(x[c]),
            "wg": wg, "wu": wu, "wd": wd,
            "wqkvT": wqkvT, "woutT": woutT, "bout": bout,
            "gam": gam, "bet": bet,
        })
    res = run_bass_kernel_spmd(nc, in_maps, core_ids=list(range(8)))
    outs = [np.asarray(res.results[c]["out"], dtype=np.float32) for c in range(8)]
    return np.stack(outs, axis=0)
